# version 02
# baseline (speedup 1.0000x reference)
import numpy as np
import ml_dtypes
import concourse.bass as bass
import concourse.mybir as mybir
from concourse.bass_utils import run_bass_kernel_spmd
from contextlib import ExitStack

F32 = mybir.dt.float32
BF16 = mybir.dt.bfloat16
I32 = mybir.dt.int32
ALU = mybir.AluOpType
AF = mybir.ActivationFunctionType
AX = mybir.AxisListType
BF = ml_dtypes.bfloat16

QMAX16 = 32767.0


def tiles(n, step):
    return [(i, min(step, n - i)) for i in range(0, n, step)]


class TBuf:
    __slots__ = ("t", "w", "r", "name")

    def __init__(self, t, name):
        self.t = t
        self.name = name
        self.w = None
        self.r = []

    def __getitem__(self, idx):
        return self.t[idx]

    @property
    def ap(self):
        return self.t[:]


class B:
    ENG = ("sync", "tensor", "vector", "scalar", "gpsimd")

    def __init__(self, name="prog"):
        self.nc = bass.Bass("TRN2", target_bir_lowering=False)
        self.name = name
        self.ctx = ExitStack()
        self.eng = {}
        self.sem = {}
        self.cnt = {e: 0 for e in self.ENG}
        self.wm = {}
        self._opened = False

    def inp(self, name, shape, dt=F32):
        return self.nc.dram_tensor(name, list(shape), dt, kind="ExternalInput").ap()

    def out(self, name, shape, dt=F32):
        return self.nc.dram_tensor(name, list(shape), dt, kind="ExternalOutput").ap()

    def dram(self, name, shape, dt=F32):
        return TBuf(self.nc.dram_tensor(name, list(shape), dt, kind="Internal"), name)

    NDMASEM = 8

    def open(self):
        assert not self._opened
        for e in self.ENG:
            self.sem[e] = self.ctx.enter_context(self.nc.semaphore("S_" + e))
        for e in ("sync", "gpsimd", "scalar"):
            for k in range(self.NDMASEM):
                key = f"{e}_dma{k}"
                self.sem[key] = self.ctx.enter_context(
                    self.nc.semaphore(f"D_{e}{k}"))
                self.cnt[key] = 0
        self.dma_rr = {e: 0 for e in ("sync", "gpsimd", "scalar")}
        self.block = self.ctx.enter_context(self.nc.Block())
        for e in self.ENG:
            self.eng[e] = getattr(self.nc, e)
        self._opened = True
        return self

    def sbuf(self, name, shape, dt=F32):
        return TBuf(self.ctx.enter_context(self.nc.sbuf_tensor(name, list(shape), dt)),
                    name)

    def psum(self, name, shape, dt=F32):
        return TBuf(self.ctx.enter_context(self.nc.psum_tensor(name, list(shape), dt)),
                    name)

    def _wait(self, eng, src, val):
        if val is None:
            return
        if src == eng == "tensor":
            return
        if src == eng:
            pass
        key = (eng, src)
        if self.wm.get(key, 0) >= val:
            return
        self.eng[eng].wait_ge(self.sem[src], val)
        self.wm[key] = val

    def _eng_of(self, src):
        return src[:-4] if src.endswith("_dma") else src

    def _collect(self, eng, reads, writes):
        need = {}

        def add(tk):
            if tk is not None and need.get(tk[0], 0) < tk[1]:
                need[tk[0]] = tk[1]

        for tb in reads:
            if tb is not None:
                add(tb.w)
        for tb in writes:
            if tb is not None:
                add(tb.w)
                for tk in tb.r:
                    add(tk)
        for sk, v in sorted(need.items()):
            self._wait(eng, sk, v)

    def _book(self, eng, ticket, reads, writes):
        for tb in writes:
            if tb is not None:
                tb.w = (eng, ticket)
                tb.r = []
        for tb in reads:
            if tb is not None:
                tb.r.append((eng, ticket))

    def emit(self, eng, fn, reads=(), writes=(), inc=1, dma=False):
        if dma:
            k = self.dma_rr[eng] % self.NDMASEM
            self.dma_rr[eng] += 1
            sem_key = f"{eng}_dma{k}"
            if self.cnt[sem_key] > 0:
                self._wait(eng, sem_key, self.cnt[sem_key])
        else:
            sem_key = eng
        self._collect(eng, reads, writes)
        inst = fn(self.eng[eng])
        self.cnt[sem_key] += inc
        inst.then_inc(self.sem[sem_key], inc)
        self._book(sem_key, self.cnt[sem_key], reads, writes)
        return self.cnt[sem_key]

    GROUP_INC_ALL = True

    def emit_many(self, eng, fns, reads=(), writes=(), inc=1):
        self._collect(eng, reads, writes)
        inst = None
        if B.GROUP_INC_ALL:
            for fn in fns:
                inst = fn(self.eng[eng])
                self.cnt[eng] += inc
                inst.then_inc(self.sem[eng], inc)
        else:
            for fn in fns:
                inst = fn(self.eng[eng])
            self.cnt[eng] += inc
            inst.then_inc(self.sem[eng], inc)
        self._book(eng, self.cnt[eng], reads, writes)
        return self.cnt[eng]

    def dma(self, queue, out, in_, reads=(), writes=(), **kw):
        out_ap = out.ap if isinstance(out, TBuf) else out
        in_ap = in_.ap if isinstance(in_, TBuf) else in_
        return self.emit(queue, lambda e: e.dma_start(out=out_ap, in_=in_ap, **kw),
                         reads=reads, writes=writes, inc=16, dma=True)

    def finish(self):
        for s in sorted(self.cnt):
            if self.cnt[s] > 0:
                self._wait("sync", s, self.cnt[s])
        self.ctx.close()
        return self.nc



def ts(b, eng, out, in0, s1, s2, op0, op1=None, reads=(), writes=()):
    if op1 is None:
        return b.emit(eng, lambda e: e.tensor_scalar(out, in0, s1, None, op0),
                      reads=reads, writes=writes)
    return b.emit(eng, lambda e: e.tensor_scalar(out, in0, s1, s2, op0, op1),
                  reads=reads, writes=writes)


def tt(b, out, in0, in1, op, reads=(), writes=()):
    return b.emit("vector", lambda e: e.tensor_tensor(out, in0, in1, op),
                  reads=reads, writes=writes)


def stt(b, out, in0, scal, in1, op0, op1, reads=(), writes=()):
    return b.emit("vector", lambda e: e.scalar_tensor_tensor(out, in0, scal, in1,
                                                             op0, op1),
                  reads=reads, writes=writes)


def vcopy(b, out, in_, reads=(), writes=()):
    return b.emit("vector", lambda e: e.tensor_copy(out, in_),
                  reads=reads, writes=writes)


def act(b, out, in_, func, bias=0.0, scale=1.0, accum=None, reads=(), writes=()):
    if accum is None:
        return b.emit("scalar", lambda e: e.activation(out, in_, func, bias=bias,
                                                       scale=scale),
                      reads=reads, writes=writes)
    return b.emit("scalar", lambda e: e.activation(out, in_, func, bias=bias,
                                                   scale=scale, accum_out=accum),
                  reads=reads, writes=writes)



class Common:
    def __init__(self, b: B, need_ident=False):
        self.b = b
        self.ones_col_bf = b.sbuf("c_ones_cb", [128, 1], BF16)
        self.ones_row_bf = b.sbuf("c_ones_rb", [1, 128], BF16)
        self.ones_row_f32 = b.sbuf("c_ones_rf", [1, 128], F32)
        b.emit("gpsimd", lambda e: e.memset(self.ones_col_bf.ap, 1.0),
               writes=[self.ones_col_bf])
        b.emit("gpsimd", lambda e: e.memset(self.ones_row_bf.ap, 1.0),
               writes=[self.ones_row_bf])
        b.emit("gpsimd", lambda e: e.memset(self.ones_row_f32.ap, 1.0),
               writes=[self.ones_row_f32])
        self.ident = None
        self.ident32 = None
        if need_ident:
            ident_in = b.inp("c_ident", [128, 128], BF16)
            self.ident = b.sbuf("c_ident_sb", [128, 128], BF16)
            b.dma("sync", self.ident, ident_in, writes=[self.ident])
            self.ident32 = b.sbuf("c_ident32", [128, 128], F32)
            vcopy(b, self.ident32.ap, self.ident.ap, reads=[self.ident],
                  writes=[self.ident32])
        self.cpool = b.sbuf("c_pool", [128, 16], F32)
        self._consts = {}
        self._cpool_used = 0

    def const_col(self, value, rows=128):
        key = (float(value), rows)
        if key in self._consts:
            return self._consts[key]
        k = self._cpool_used
        assert k < 16
        self._cpool_used += 1
        ap = self.cpool[0:rows, k:k + 1]
        self.b.emit("gpsimd",
                    lambda e: e.memset(self.cpool[:, k:k + 1], float(value)),
                    reads=[self.cpool], writes=[self.cpool])
        self._consts[key] = ap
        return ap


def bcast_row_f32(b, com, mov_ap, out_ps, reads=(), nrows=128):
    return b.emit("tensor",
                  lambda e: e.matmul(out_ps[0:nrows, 0:mov_ap.shape[-1]],
                                     com.ones_row_f32[:, 0:nrows], mov_ap,
                                     start=True, stop=True),
                  reads=[com.ones_row_f32, *reads], writes=[out_ps] if isinstance(out_ps, TBuf) else ())


def bcast_row_f32_t(b, com, mov_tb, mov_ap, out_tb, out_rows=128):
    n = mov_ap.shape[-1]
    t = None
    for (c0, cl) in tiles(n, 512):
        t = b.emit("tensor",
                   lambda e, c0=c0, cl=cl: e.matmul(
                       out_tb[0:out_rows, c0:c0 + cl],
                       com.ones_row_f32[:, 0:out_rows], mov_ap[:, c0:c0 + cl],
                       start=True, stop=True),
                   reads=[com.ones_row_f32, mov_tb], writes=[out_tb])
    return t


def bcast_row_bf(b, com, mov_tb, mov_ap, out_tb, out_rows=128):
    n = mov_ap.shape[-1]
    t = None
    for (c0, cl) in tiles(n, 512):
        t = b.emit("tensor",
                   lambda e, c0=c0, cl=cl: e.matmul(
                       out_tb[0:out_rows, c0:c0 + cl],
                       com.ones_row_bf[:, 0:out_rows], mov_ap[:, c0:c0 + cl],
                       start=True, stop=True),
                   reads=[com.ones_row_bf, mov_tb], writes=[out_tb])
    return t


def partition_max(b, com, vec_tb, vec_ap, scr, ps, out_tb, out_ap):
    b.emit("gpsimd", lambda e: e.memset(scr.ap, 0.0), writes=[scr])
    vcopy(b, scr[:, 0:1], vec_ap, reads=[vec_tb], writes=[scr])
    b.emit("tensor", lambda e: e.transpose(ps.ap, scr.ap, com.ident32.ap),
           reads=[scr, com.ident32], writes=[ps])
    b.emit("vector", lambda e: e.reduce_max(out_ap, ps[0:1, :], axis=AX.X),
           reads=[ps], writes=[out_tb])



class LNWork:
    def __init__(self, b, tag):
        self.xbf = b.sbuf(f"{tag}_xbf", [128, 512], BF16)
        self.x2bf = b.sbuf(f"{tag}_x2bf", [128, 512], BF16)
        self.bc_r = b.psum(f"{tag}_bcr", [128, 512], F32)
        self.bc_b = b.psum(f"{tag}_bcb", [128, 512], F32)
        self.ps_s = self.bc_r
        self.ps_q = self.bc_b
        self.row_mu = b.sbuf(f"{tag}_rmu", [1, 512], F32)
        self.row_b = b.sbuf(f"{tag}_rb", [1, 512], F32)
        self.row_r = b.sbuf(f"{tag}_rr", [1, 512], F32)


def emit_ln_block(b, com, w: LNWork, x_tiles, x_aps, out_tiles, out_aps, eps=1e-5):
    nf = len(x_tiles)
    d = nf * 128
    tl = x_aps[0].shape[-1]
    for i in range(nf):
        vcopy(b, w.xbf[:, 0:tl], x_aps[i], reads=[x_tiles[i]], writes=[w.xbf])
        act(b, w.x2bf[:, 0:tl], x_aps[i], AF.Square, reads=[x_tiles[i]],
            writes=[w.x2bf])
        b.emit("tensor", lambda e, i=i: e.matmul(w.ps_s[0:1, 0:tl], com.ones_col_bf.ap,
                                                 w.xbf[:, 0:tl], start=(i == 0),
                                                 stop=(i == nf - 1)),
               reads=[w.xbf, com.ones_col_bf], writes=[w.ps_s])
        b.emit("tensor", lambda e, i=i: e.matmul(w.ps_q[0:1, 0:tl], com.ones_col_bf.ap,
                                                 w.x2bf[:, 0:tl], start=(i == 0),
                                                 stop=(i == nf - 1)),
               reads=[w.x2bf, com.ones_col_bf], writes=[w.ps_q])
    ts(b, "vector", w.row_mu[:, 0:tl], w.ps_s[0:1, 0:tl], 1.0 / d, None, ALU.mult,
       reads=[w.ps_s], writes=[w.row_mu])
    stt(b, w.row_b[:, 0:tl], w.row_mu[:, 0:tl], -1.0, w.row_mu[:, 0:tl],
        ALU.mult, ALU.mult, reads=[w.row_mu], writes=[w.row_b])
    b.emit("vector", lambda e: e.scalar_tensor_tensor(
        w.row_r[:, 0:tl], w.ps_q[0:1, 0:tl], 1.0 / d, w.row_b[:, 0:tl],
        ALU.mult, ALU.add), reads=[w.ps_q, w.row_b], writes=[w.row_r])
    act(b, w.row_r[:, 0:tl], w.row_r[:, 0:tl], AF.Ln,
        bias=com.const_col(eps, rows=1), scale=1.0, reads=[w.row_r], writes=[w.row_r])
    act(b, w.row_r[:, 0:tl], w.row_r[:, 0:tl], AF.Exp, bias=0.0, scale=-0.5,
        reads=[w.row_r], writes=[w.row_r])
    stt(b, w.row_b[:, 0:tl], w.row_mu[:, 0:tl], -1.0, w.row_r[:, 0:tl],
        ALU.mult, ALU.mult, reads=[w.row_mu, w.row_r], writes=[w.row_b])
    bcast_row_f32_t(b, com, w.row_r, w.row_r[:, 0:tl], w.bc_r)
    bcast_row_f32_t(b, com, w.row_b, w.row_b[:, 0:tl], w.bc_b)
    for i in range(nf):
        tt(b, out_aps[i], x_aps[i], w.bc_r[0:128, 0:tl], ALU.mult,
           reads=[x_tiles[i], w.bc_r], writes=[out_tiles[i]])
        tt(b, out_aps[i], out_aps[i], w.bc_b[0:128, 0:tl], ALU.add,
           reads=[out_tiles[i], w.bc_b], writes=[out_tiles[i]])



def emit_mm(b, stat_fn, n_d, mov_fn, t_len, o_list, pool, evac_fn, t_step=512):
    all_tblks = tiles(t_len, t_step)
    npool = len(pool)
    idx = 0
    for oi in o_list:
        for c0 in range(0, len(all_tblks), npool):
            tblks = all_tblks[c0:c0 + npool]
            pss = [pool[(idx + k) % npool] for k in range(len(tblks))]
            idx += len(tblks)
            reads = []
            fns = []
            for d in range(n_d):
                s_tb, s_ap = stat_fn(d, oi)
                reads.append(s_tb)
                for k, (t0, tl) in enumerate(tblks):
                    m_tb, m_ap = mov_fn(d, t0, tl)
                    reads.append(m_tb)
                    fns.append(lambda e, s_ap=s_ap, m_ap=m_ap, ps=pss[k], tl=tl,
                               d=d: e.matmul(ps[:, 0:tl], s_ap, m_ap,
                                             start=(d == 0), stop=(d == n_d - 1)))
            b.emit_many("tensor", fns, reads=[t for t in dict.fromkeys(reads)],
                        writes=list(dict.fromkeys(pss)))
            for k, (t0, tl) in enumerate(tblks):
                evac_fn(oi, t0, tl, pss[k])



def small_recip(b, out_ap, in_ap, reads=(), writes=()):
    return b.emit("vector", lambda e: e.reciprocal(out_ap, in_ap),
                  reads=reads, writes=writes)



_PROG = {}


def _run(key, builder, in_maps):
    import concourse.bass_utils as _bu
    if key not in _PROG:
        _PROG[key] = builder()
    res = _bu.run_bass_kernel_spmd(_PROG[key], in_maps,
                                   core_ids=list(range(len(in_maps))))
    return res.results


IDENT_BF = np.eye(128).astype(BF)



def build_dequant():
    b = B("dequant").open()
    specs = [("g0", 384, 1024), ("u0", 384, 1024), ("d0", 128, 3072),
             ("g1", 384, 1024), ("u1", 384, 1024), ("d1", 128, 3072)]
    ins, outs = {}, {}
    for nm, r, i in specs:
        ins[nm] = b.inp("w_" + nm, [r, i], F32)
        outs[nm] = b.out("q_" + nm, [r, i], BF16)
    com = Common(b)
    wt = b.sbuf("wt", [128, 3072], F32)
    s_raw = b.sbuf("s_raw", [128, 96], F32)
    s_s = b.sbuf("s_s", [128, 96], F32)
    s_inv = b.sbuf("s_inv", [128, 96], F32)
    t1 = b.sbuf("t1", [128, 3072], F32)
    ti = b.sbuf("ti", [128, 3072], I32)
    tf = b.sbuf("tf", [128, 3072], F32)
    m1 = b.sbuf("m1", [128, 3072], F32)
    m2 = b.sbuf("m2", [128, 3072], F32)
    ob = b.sbuf("ob", [128, 3072], BF16)
    for nm, r, i in specs:
        nblk = i // 32
        for rt in range(r // 128):
            w_in = ins[nm][rt * 128:(rt + 1) * 128, :]
            b.dma("sync", wt[:, 0:i], w_in, writes=[wt])
            v3 = wt[:, 0:i].rearrange("p (a c) -> p a c", c=32)
            b.emit("vector", lambda e, v3=v3, nblk=nblk: e.tensor_reduce(
                s_raw[:, 0:nblk].rearrange("p (a c) -> p a c", c=1), v3,
                axis=AX.X, op=ALU.max, apply_absolute_value=True),
                reads=[wt], writes=[s_raw])
            ts(b, "vector", s_s[:, 0:nblk], s_raw[:, 0:nblk], 1.0 / 7.0, 1e-12,
               ALU.mult, ALU.max, reads=[s_raw], writes=[s_s])
            small_recip(b, s_inv[:, 0:nblk], s_s[:, 0:nblk], reads=[s_s],
                        writes=[s_inv])
            ibc = s_inv[:, 0:nblk].rearrange("p (a c) -> p a c", c=1) \
                .broadcast_to([128, nblk, 32])
            sbc = s_s[:, 0:nblk].rearrange("p (a c) -> p a c", c=1) \
                .broadcast_to([128, nblk, 32])
            v1 = t1[:, 0:i].rearrange("p (a c) -> p a c", c=32)
            b.emit("vector", lambda e, v3=v3, v1=v1, ibc=ibc: e.tensor_tensor(
                v1, v3, ibc, ALU.mult), reads=[wt, s_inv], writes=[t1])
            vcopy(b, ti[:, 0:i], t1[:, 0:i], reads=[t1], writes=[ti])
            vcopy(b, tf[:, 0:i], ti[:, 0:i], reads=[ti], writes=[tf])
            stt(b, m1[:, 0:i], tf[:, 0:i], 0.5, t1[:, 0:i], ALU.add, ALU.is_le,
                reads=[tf, t1], writes=[m1])
            stt(b, m2[:, 0:i], tf[:, 0:i], -0.5, t1[:, 0:i], ALU.add, ALU.is_ge,
                reads=[tf, t1], writes=[m2])
            tt(b, tf[:, 0:i], tf[:, 0:i], m1[:, 0:i], ALU.add, reads=[tf, m1],
               writes=[tf])
            tt(b, tf[:, 0:i], tf[:, 0:i], m2[:, 0:i], ALU.subtract,
               reads=[tf, m2], writes=[tf])
            v2 = tf[:, 0:i].rearrange("p (a c) -> p a c", c=32)
            vo = ob[:, 0:i].rearrange("p (a c) -> p a c", c=32)
            b.emit("vector", lambda e, vo=vo, v2=v2, sbc=sbc: e.tensor_tensor(
                vo, v2, sbc, ALU.mult), reads=[tf, s_s], writes=[ob])
            b.dma("sync", outs[nm][rt * 128:(rt + 1) * 128, :], ob[:, 0:i],
                  reads=[ob])
    return b.finish()


def host_dequant(inputs):
    names = {"g0": "L0_ffn_w_gate", "u0": "L0_ffn_w_up", "d0": "L0_ffn_w_down",
             "g1": "L1_ffn_w_gate", "u1": "L1_ffn_w_up", "d1": "L1_ffn_w_down"}
    in_maps = []
    for c in range(8):
        m = {}
        for nm, full in names.items():
            w = inputs[full]
            r = w.shape[0] // 8
            m["w_" + nm] = np.ascontiguousarray(w[c * r:(c + 1) * r, :])
        in_maps.append(m)
    res = _run("dequant", build_dequant, in_maps)
    out = {}
    for nm in names:
        full = np.concatenate([np.asarray(res[c]["q_" + nm]) for c in range(8)],
                              axis=0)
        out[nm + "T"] = np.ascontiguousarray(full.T)
    return out



def build_ffn_ln():
    b = B("ffn_ln").open()
    x = b.inp("x", [1024, 2048], F32)
    lnx = b.out("lnx", [1024, 2048], F32)
    pmax = b.out("pmax", [1, 1], F32)
    com = Common(b, need_ident=True)
    w = LNWork(b, "ln")
    xin = [b.sbuf(f"xin{i}", [128, 512], F32) for i in range(8)]
    st = [b.sbuf(f"st{i}", [128, 512], F32) for i in range(8)]
    mx = b.sbuf("mx", [128, 1], F32)
    mxc = b.sbuf("mxc", [128, 1], F32)
    scr = b.sbuf("scr", [128, 128], F32)
    psx = b.psum("psx", [128, 128], F32)
    pm_sb = b.sbuf("pm_sb", [1, 1], F32)
    b.emit("gpsimd", lambda e: e.memset(mx.ap, 0.0), writes=[mx])
    for (t0, tl) in tiles(2048, 512):
        for i in range(8):
            b.dma("sync", xin[i].ap, x[i * 128:(i + 1) * 128, t0:t0 + tl],
                  writes=[xin[i]])
        emit_ln_block(b, com, w, xin, [t.ap for t in xin], st, [t.ap for t in st])
        for i in range(8):
            b.emit("vector", lambda e, i=i: e.reduce_max(
                mxc.ap, st[i].ap, axis=AX.X, apply_absolute_value=True),
                reads=[st[i]], writes=[mxc])
            tt(b, mx.ap, mx.ap, mxc.ap, ALU.max, reads=[mx, mxc], writes=[mx])
            b.dma("sync", lnx[i * 128:(i + 1) * 128, t0:t0 + tl], st[i].ap,
                  reads=[st[i]])
    partition_max(b, com, mx, mx.ap, scr, psx, pm_sb, pm_sb.ap)
    b.dma("sync", pmax, pm_sb.ap, reads=[pm_sb])
    return b.finish()


def _qdq_setup(b, com, gmax_name, extra_cols=0):
    g_in = b.inp(gmax_name, [1, 8], F32)
    g8 = b.sbuf(gmax_name + "_8", [1, 8], F32)
    g1 = b.sbuf(gmax_name + "_1", [1, 4], F32)
    ps = b.psum(gmax_name + "_ps", [128, 2], F32)
    sc = b.sbuf(gmax_name + "_sc", [128, 2], F32)
    b.dma("sync", g8, g_in, writes=[g8])
    b.emit("vector", lambda e: e.reduce_max(g1[:, 3:4], g8.ap, axis=AX.X),
           reads=[g8], writes=[g1])
    ts(b, "vector", g1[:, 0:1], g1[:, 3:4], 1.0 / QMAX16, None, ALU.mult,
       reads=[g1], writes=[g1])
    small_recip(b, g1[:, 1:2], g1[:, 0:1], reads=[g1], writes=[g1])
    bcast_row_f32_t(b, com, g1, g1[:, 0:2], ps)
    vcopy(b, sc.ap, ps[0:128, 0:2], reads=[ps], writes=[sc])
    return sc


def build_ffn_ug():
    b = B("ffn_ug").open()
    lnx = b.inp("lnx", [1024, 2048], F32)
    wuT = b.inp("wuT", [1024, 3072], BF16)
    wgT = b.inp("wgT", [1024, 3072], BF16)
    upr = b.out("upr", [3072, 2048], BF16)
    gater = b.out("gater", [3072, 2048], BF16)
    pmu = b.out("pmax_u", [1, 1], F32)
    pmg = b.out("pmax_g", [1, 1], F32)
    com = Common(b, need_ident=True)
    sc = _qdq_setup(b, com, "gmax")
    wu = [b.sbuf(f"wu{d}", [128, 3072], BF16) for d in range(8)]
    for d in range(8):
        b.dma("sync", wu[d].ap, wuT[d * 128:(d + 1) * 128, :], writes=[wu[d]])
    xq = [b.sbuf(f"xq{d}", [128, 2048], BF16) for d in range(8)]
    xl = [b.sbuf(f"xl{k}", [128, 2048], F32) for k in range(2)]
    tf = b.sbuf("tf", [128, 2048], F32)
    ti = b.sbuf("ti", [128, 2048], I32)
    for d in range(8):
        b.dma("sync", xl[d % 2].ap, lnx[d * 128:(d + 1) * 128, :],
              writes=[xl[d % 2]])
        ts(b, "vector", tf.ap, xl[d % 2].ap, sc[:, 1:2], None, ALU.mult,
           reads=[xl[d % 2], sc], writes=[tf])
        vcopy(b, ti.ap, tf.ap, reads=[tf], writes=[ti])
        vcopy(b, xq[d].ap, ti.ap, reads=[ti], writes=[xq[d]])
    pool = [b.psum(f"mmp{k}", [128, 512], F32) for k in range(4)]
    stage = [b.sbuf(f"stg{k}", [128, 2048], BF16) for k in range(2)]
    mxu = b.sbuf("mxu", [128, 1], F32)
    mxg = b.sbuf("mxg", [128, 1], F32)
    mxc = b.sbuf("mxc2", [128, 1], F32)
    b.emit("gpsimd", lambda e: e.memset(mxu.ap, 0.0), writes=[mxu])
    b.emit("gpsimd", lambda e: e.memset(mxg.ap, 0.0), writes=[mxg])
    scr = b.sbuf("scr", [128, 128], F32)
    psx = b.psum("psx", [128, 128], F32)
    pm_sb = b.sbuf("pm_sb", [1, 2], F32)

    def run_side(wlist, out_dram, mxt):
        def evac(oi, t0, tl, ps):
            sg = stage[oi % 2]
            ts(b, "vector", sg[:, t0:t0 + tl], ps[:, 0:tl], sc[:, 0:1], None,
               ALU.mult, reads=[ps, sc], writes=[sg])
            if t0 + tl == 2048:
                b.emit("vector", lambda e: e.reduce_max(
                    mxc.ap, sg.ap, axis=AX.X, apply_absolute_value=True),
                    reads=[sg], writes=[mxc])
                tt(b, mxt.ap, mxt.ap, mxc.ap, ALU.max, reads=[mxt, mxc],
                   writes=[mxt])
                b.dma("sync", out_dram[oi * 128:(oi + 1) * 128, :], sg.ap,
                      reads=[sg])
        emit_mm(b,
                lambda d, oi: (wlist[d], wlist[d][:, oi * 128:(oi + 1) * 128]),
                8,
                lambda d, t0, tl: (xq[d], xq[d][:, t0:t0 + tl]),
                2048, list(range(24)), pool, evac)

    run_side(wu, upr, mxu)
    for d in range(8):
        b.dma("sync", wu[d].ap, wgT[d * 128:(d + 1) * 128, :], writes=[wu[d]])
    run_side(wu, gater, mxg)
    partition_max(b, com, mxu, mxu.ap, scr, psx, pm_sb, pm_sb[:, 0:1])
    partition_max(b, com, mxg, mxg.ap, scr, psx, pm_sb, pm_sb[:, 1:2])
    b.dma("sync", pmu, pm_sb[:, 0:1], reads=[pm_sb])
    b.dma("sync", pmg, pm_sb[:, 1:2], reads=[pm_sb])
    return b.finish()


def build_ffn_p1():
    b = B("ffn_p1").open()
    gater = b.inp("gater", [3072, 2048], BF16)
    p1 = b.out("p1", [3072, 2048], BF16)
    pmp = b.out("pmax_p", [1, 1], F32)
    com = Common(b, need_ident=True)
    sc = _qdq_setup(b, com, "gmax_g")
    gl = [b.sbuf(f"gl{k}", [128, 2048], BF16) for k in range(2)]
    tf = b.sbuf("tf", [128, 2048], F32)
    ti = b.sbuf("ti", [128, 2048], I32)
    rf = b.sbuf("rf", [128, 2048], F32)
    sg = b.sbuf("sg", [128, 2048], F32)
    ti2 = b.sbuf("ti2", [128, 2048], I32)
    qf = b.sbuf("qf", [128, 2048], F32)
    ob = [b.sbuf(f"ob{k}", [128, 2048], BF16) for k in range(2)]
    mx = b.sbuf("mx", [128, 1], F32)
    mxc = b.sbuf("mxc", [128, 1], F32)
    scr = b.sbuf("scr", [128, 128], F32)
    psx = b.psum("psx", [128, 128], F32)
    pm_sb = b.sbuf("pm_sb", [1, 1], F32)
    b.emit("gpsimd", lambda e: e.memset(mx.ap, 0.0), writes=[mx])
    for oi in range(24):
        g = gl[oi % 2]
        o = ob[oi % 2]
        b.dma("sync", g.ap, gater[oi * 128:(oi + 1) * 128, :], writes=[g])
        ts(b, "vector", tf.ap, g.ap, sc[:, 1:2], None, ALU.mult,
           reads=[g, sc], writes=[tf])
        vcopy(b, ti.ap, tf.ap, reads=[ti, tf], writes=[ti])
        vcopy(b, rf.ap, ti.ap, reads=[ti], writes=[rf])
        act(b, sg.ap, rf.ap, AF.Sigmoid, bias=0.0, scale=sc[:, 0:1],
            reads=[rf, sc], writes=[sg])
        ts(b, "vector", qf.ap, sg.ap, 65536.0, None, ALU.mult,
           reads=[sg], writes=[qf])
        vcopy(b, ti2.ap, qf.ap, reads=[qf], writes=[ti2])
        vcopy(b, qf.ap, ti2.ap, reads=[ti2], writes=[qf])
        ts(b, "vector", qf.ap, qf.ap, 65535.0, None, ALU.min,
           reads=[qf], writes=[qf])
        tt(b, o.ap, rf.ap, qf.ap, ALU.mult, reads=[rf, qf], writes=[o])
        b.emit("vector", lambda e, o=o: e.reduce_max(
            mxc.ap, o.ap, axis=AX.X, apply_absolute_value=True),
            reads=[o], writes=[mxc])
        tt(b, mx.ap, mx.ap, mxc.ap, ALU.max, reads=[mx, mxc], writes=[mx])
        b.dma("sync", p1[oi * 128:(oi + 1) * 128, :], o.ap, reads=[o])
    partition_max(b, com, mx, mx.ap, scr, psx, pm_sb, pm_sb.ap)
    b.dma("sync", pmp, pm_sb.ap, reads=[pm_sb])
    return b.finish()


def build_ffn_o():
    b = B("ffn_o").open()
    p1 = b.inp("p1", [3072, 2048], BF16)
    upr = b.inp("upr", [3072, 2048], BF16)
    p2 = b.out("p2", [3072, 2048], BF16)
    pmo = b.out("pmax_o", [1, 1], F32)
    com = Common(b, need_ident=True)
    sc_p = _qdq_setup(b, com, "pmax_p")
    sc_u = _qdq_setup(b, com, "gmax_u")
    pl = [b.sbuf(f"pl{k}", [128, 2048], BF16) for k in range(2)]
    ul = [b.sbuf(f"ul{k}", [128, 2048], BF16) for k in range(2)]
    tf = b.sbuf("tf", [128, 2048], F32)
    ti = b.sbuf("ti", [128, 2048], I32)
    g2 = b.sbuf("g2", [128, 2048], F32)
    ru = b.sbuf("ru", [128, 2048], F32)
    ob = [b.sbuf(f"ob{k}", [128, 2048], BF16) for k in range(2)]
    mx = b.sbuf("mx", [128, 1], F32)
    mxc = b.sbuf("mxc", [128, 1], F32)
    scr = b.sbuf("scr", [128, 128], F32)
    psx = b.psum("psx", [128, 128], F32)
    pm_sb = b.sbuf("pm_sb", [1, 1], F32)
    b.emit("gpsimd", lambda e: e.memset(mx.ap, 0.0), writes=[mx])
    for oi in range(24):
        p = pl[oi % 2]
        u = ul[oi % 2]
        o = ob[oi % 2]
        b.dma("sync", p.ap, p1[oi * 128:(oi + 1) * 128, :], writes=[p])
        b.dma("sync", u.ap, upr[oi * 128:(oi + 1) * 128, :], writes=[u])
        ts(b, "vector", tf.ap, p.ap, sc_p[:, 1:2], None, ALU.mult,
           reads=[p, sc_p], writes=[tf])
        vcopy(b, ti.ap, tf.ap, reads=[tf], writes=[ti])
        vcopy(b, g2.ap, ti.ap, reads=[ti], writes=[g2])
        ts(b, "vector", tf.ap, u.ap, sc_u[:, 1:2], None, ALU.mult,
           reads=[u, sc_u], writes=[tf])
        vcopy(b, ti.ap, tf.ap, reads=[tf], writes=[ti])
        vcopy(b, ru.ap, ti.ap, reads=[ti], writes=[ru])
        tt(b, o.ap, g2.ap, ru.ap, ALU.mult, reads=[g2, ru], writes=[o])
        b.emit("vector", lambda e, o=o: e.reduce_max(
            mxc.ap, o.ap, axis=AX.X, apply_absolute_value=True),
            reads=[o], writes=[mxc])
        tt(b, mx.ap, mx.ap, mxc.ap, ALU.max, reads=[mx, mxc], writes=[mx])
        b.dma("sync", p2[oi * 128:(oi + 1) * 128, :], o.ap, reads=[o])
    partition_max(b, com, mx, mx.ap, scr, psx, pm_sb, pm_sb.ap)
    b.dma("sync", pmo, pm_sb.ap, reads=[pm_sb])
    return b.finish()


def build_ffn_down():
    b = B("ffn_down").open()
    p2 = b.inp("p2", [3072, 2048], BF16)
    wdT = b.inp("wdT", [3072, 1024], BF16)
    x = b.inp("x", [1024, 2048], F32)
    mo_in = b.inp("pmax_o", [1, 8], F32)
    mp_in = b.inp("pmax_p", [1, 8], F32)
    mg_in = b.inp("gmax_g", [1, 8], F32)
    mu_in = b.inp("gmax_u", [1, 8], F32)
    xnew = b.out("xnew", [1024, 2048], F32)
    com = Common(b)
    row = b.sbuf("row", [1, 12], F32)
    m8 = b.sbuf("m8", [1, 8], F32)
    for k, src in enumerate([mo_in, mp_in, mg_in, mu_in]):
        b.dma("sync", m8, src, writes=[m8])
        b.emit("vector", lambda e, k=k: e.reduce_max(row[:, k:k + 1], m8.ap,
                                                     axis=AX.X),
               reads=[m8], writes=[row])
    small_recip(b, row[:, 4:5], row[:, 0:1], reads=[row], writes=[row])
    ts(b, "vector", row[:, 4:5], row[:, 4:5], QMAX16, None, ALU.mult,
       reads=[row], writes=[row])
    tt(b, row[:, 5:6], row[:, 0:1], row[:, 1:2], ALU.mult, reads=[row], writes=[row])
    tt(b, row[:, 5:6], row[:, 5:6], row[:, 2:3], ALU.mult, reads=[row], writes=[row])
    tt(b, row[:, 5:6], row[:, 5:6], row[:, 3:4], ALU.mult, reads=[row], writes=[row])
    ts(b, "vector", row[:, 5:6], row[:, 5:6], float(2.0 ** -16 / QMAX16 ** 4),
       None, ALU.mult, reads=[row], writes=[row])
    ps_sc = b.psum("ps_sc", [128, 2], F32)
    sc = b.sbuf("sc", [128, 2], F32)
    bcast_row_f32_t(b, com, row, row[:, 4:6], ps_sc)
    vcopy(b, sc.ap, ps_sc[0:128, 0:2], reads=[ps_sc], writes=[sc])
    wd = [b.sbuf(f"wd{d}", [128, 1024], BF16) for d in range(24)]
    for d in range(24):
        b.dma("sync", wd[d].ap, wdT[d * 128:(d + 1) * 128, :], writes=[wd[d]])
    orb = [b.sbuf(f"orb{d}", [128, 2048], BF16) for d in range(24)]
    pload = [b.sbuf(f"pload{k}", [128, 2048], BF16) for k in range(2)]
    tf = b.sbuf("tf", [128, 2048], F32)
    ti = b.sbuf("ti", [128, 2048], I32)
    for d in range(24):
        b.dma("sync", pload[d % 2].ap, p2[d * 128:(d + 1) * 128, :],
              writes=[pload[d % 2]])
        ts(b, "vector", tf.ap, pload[d % 2].ap, sc[:, 0:1], None, ALU.mult,
           reads=[pload[d % 2], sc], writes=[tf])
        vcopy(b, ti.ap, tf.ap, reads=[tf], writes=[ti])
        vcopy(b, orb[d].ap, ti.ap, reads=[ti], writes=[orb[d]])
    pool = [b.psum(f"mmp{k}", [128, 512], F32) for k in range(4)]
    xl = [b.sbuf(f"xl{k}", [128, 512], F32) for k in range(4)]
    so = [b.sbuf(f"so{k}", [128, 512], F32) for k in range(4)]

    def evac(oi, t0, tl, ps):
        k = (t0 // 512) % 4
        b.dma("sync", xl[k].ap, x[oi * 128:(oi + 1) * 128, t0:t0 + tl],
              writes=[xl[k]])
        stt(b, so[k].ap, ps[:, 0:tl], sc[:, 1:2], xl[k].ap, ALU.mult, ALU.add,
            reads=[ps, sc, xl[k]], writes=[so[k]])
        b.dma("sync", xnew[oi * 128:(oi + 1) * 128, t0:t0 + tl], so[k].ap,
              reads=[so[k]])

    emit_mm(b, lambda d, oi: (wd[d], wd[d][:, oi * 128:(oi + 1) * 128]), 24,
            lambda d, t0, tl: (orb[d], orb[d][:, t0:t0 + tl]),
            2048, list(range(8)), pool, evac)
    return b.finish()



def host_ffn(xs, wq):
    r1 = _run("ffn_ln", build_ffn_ln, [dict(x=xs[c], c_ident=IDENT_BF)
                                       for c in range(8)])
    gmax = np.concatenate([np.asarray(r1[c]["pmax"]) for c in range(8)],
                          axis=1).astype(np.float32)
    lnx = [np.asarray(r1[c]["lnx"]) for c in range(8)]
    r2 = _run("ffn_ug", build_ffn_ug,
              [dict(lnx=lnx[c], gmax=gmax, wuT=wq["uT"], wgT=wq["gT"],
                    c_ident=IDENT_BF) for c in range(8)])
    gmax_u = np.concatenate([np.asarray(r2[c]["pmax_u"]) for c in range(8)], axis=1)
    gmax_g = np.concatenate([np.asarray(r2[c]["pmax_g"]) for c in range(8)], axis=1)
    r3 = _run("ffn_p1", build_ffn_p1,
              [dict(gater=np.asarray(r2[c]["gater"]), gmax_g=gmax_g,
                    c_ident=IDENT_BF) for c in range(8)])
    pmax_p = np.concatenate([np.asarray(r3[c]["pmax_p"]) for c in range(8)], axis=1)
    r4 = _run("ffn_o", build_ffn_o,
              [dict(p1=np.asarray(r3[c]["p1"]), upr=np.asarray(r2[c]["upr"]),
                    pmax_p=pmax_p, gmax_u=gmax_u, c_ident=IDENT_BF)
               for c in range(8)])
    pmax_o = np.concatenate([np.asarray(r4[c]["pmax_o"]) for c in range(8)], axis=1)
    r5 = _run("ffn_down", build_ffn_down,
              [dict(p2=np.asarray(r4[c]["p2"]), wdT=wq["dT"], x=xs[c],
                    pmax_o=pmax_o, pmax_p=pmax_p, gmax_g=gmax_g, gmax_u=gmax_u)
               for c in range(8)])
    return [np.asarray(r5[c]["xnew"]) for c in range(8)]



ATTN_A_DEBUG = False


def build_attn_a():
    b = B("attn_a").open()
    TE = 2304
    x_ext = b.inp("x_ext", [1024, TE], F32)
    x_res = b.inp("x_res", [1024, 2048], F32)
    qwT = b.inp("qwT", [1024, 1024], F32)
    kwT = b.inp("kwT", [1024, 1024], F32)
    vwT = b.inp("vwT", [1024, 1024], F32)
    owT = b.inp("owT", [1024, 1024], F32)
    qb_in = b.inp("qb", [128, 8], F32)
    kb_in = b.inp("kb", [128, 8], F32)
    vb_in = b.inp("vb", [128, 8], F32)
    ob_in = b.inp("ob", [128, 8], F32)
    mask_in = b.inp("mask", [128, 2048], F32)
    xnew = b.out("xnew", [1024, 2048], F32)
    com = Common(b, need_ident=True)
    w = LNWork(b, "ln")

    bias = {}
    for nm, src in [("q", qb_in), ("k", kb_in), ("v", vb_in), ("o", ob_in)]:
        t = b.sbuf(f"bias_{nm}", [128, 8], F32)
        b.dma("sync", t.ap, src, writes=[t])
        bias[nm] = t
    mask = b.sbuf("mask_sb", [128, 2048], F32)
    b.dma("sync", mask, mask_in, writes=[mask])

    lnbf = [b.sbuf(f"lnbf{i}", [128, TE], BF16) for i in range(8)]
    xin = [b.sbuf(f"xin{i}", [128, 512], F32) for i in range(8)]
    for (t0, tl) in tiles(TE, 512):
        for i in range(8):
            b.dma("sync", xin[i][:, 0:tl], x_ext[i * 128:(i + 1) * 128, t0:t0 + tl],
                  writes=[xin[i]])
        emit_ln_block(b, com, w, xin, [t[:, 0:tl] for t in xin], lnbf,
                      [t[:, t0:t0 + tl] for t in lnbf])

    wsb = [b.sbuf(f"wsb{d}", [128, 1024], BF16) for d in range(8)]
    QT = [b.sbuf(f"QT{i}", [128, 2048], BF16) for i in range(8)]
    KT = [b.sbuf(f"KT{i}", [128, TE], BF16) for i in range(8)]
    VT = [b.sbuf(f"VT{i}", [128, TE], BF16) for i in range(8)]
    pool = [b.psum(f"mmp{k}", [128, 512], F32) for k in range(2)]

    def proj(wT, dst, t_lo, t_len, bias_t, scale_after):
        for d in range(8):
            b.dma("gpsimd", wsb[d].ap, wT[d * 128:(d + 1) * 128, :],
                  writes=[wsb[d]])

        def evac(oi, t0, tl, ps):
            ts(b, "vector", dst[oi][:, t0:t0 + tl], ps[:, 0:tl],
               bias_t[:, oi:oi + 1], scale_after, ALU.add,
               ALU.mult if scale_after is not None else None,
               reads=[ps, bias_t], writes=[dst[oi]])
        emit_mm(b, lambda d, oi: (wsb[d], wsb[d][:, oi * 128:(oi + 1) * 128]), 8,
                lambda d, t0, tl: (lnbf[d], lnbf[d][:, t_lo + t0:t_lo + t0 + tl]),
                t_len, list(range(8)), pool, evac)

    proj(qwT, QT, 256, 2048, bias["q"], 0.125)
    proj(kwT, KT, 0, TE, bias["k"], None)
    proj(vwT, VT, 0, TE, bias["v"], None)

    aout = lnbf
    s_f = b.sbuf("s_f", [128, 512], F32)
    mrow = b.sbuf("mrow", [128, 1], F32)
    negm = b.sbuf("negm", [128, 1], F32)
    den = b.sbuf("den", [128, 1], F32)
    rden = b.sbuf("rden", [128, 1], F32)
    p_bf = b.sbuf("p_bf", [128, 512], BF16)
    pn_bf = b.sbuf("pn_bf", [128, 512], BF16)
    pt_ps = b.psum("pt_ps", [128, 512], BF16)
    pt_sb = b.sbuf("pt_sb", [128, 512], BF16)
    vt_ps = b.psum("vt_ps", [128, 64], BF16)
    vtok = b.sbuf("vtok", [128, 256], BF16)
    ps_s = b.psum("ps_sc", [128, 512], F32)
    ps_o = b.psum("ps_o", [64, 128], F32)

    for h in range(16):
        ti_, r0 = h // 2, (h % 2) * 64
        for qt in range(16):
            wstart = (qt // 2) * 256
            q0 = qt * 128
            b.emit("tensor", lambda e, ti_=ti_, r0=r0, q0=q0, ws=wstart: e.matmul(
                ps_s.ap, QT[ti_][r0:r0 + 64, q0:q0 + 128],
                KT[ti_][r0:r0 + 64, ws:ws + 512], start=True, stop=True,
                tile_position=(r0, 0)),
                reads=[QT[ti_], KT[ti_]], writes=[ps_s])
            mv = (qt % 2) if qt >= 2 else (2 + qt)
            tt(b, s_f.ap, ps_s.ap, mask[:, mv * 512:(mv + 1) * 512], ALU.add,
               reads=[ps_s, mask], writes=[s_f])
            b.emit("vector", lambda e: e.reduce_max(mrow.ap, s_f.ap, axis=AX.X),
                   reads=[s_f], writes=[mrow])
            ts(b, "vector", negm.ap, mrow.ap, -1.0, None, ALU.mult,
               reads=[mrow], writes=[negm])
            act(b, p_bf.ap, s_f.ap, AF.Exp, bias=negm[0:128, 0:1], scale=1.0,
                accum=den.ap, reads=[s_f, negm], writes=[p_bf, den])
            small_recip(b, rden.ap, den.ap, reads=[den], writes=[rden])
            ts(b, "vector", pn_bf.ap, p_bf.ap, rden[0:128, 0:1], None, ALU.mult,
               reads=[p_bf, rden], writes=[pn_bf])
            for j in range(4):
                b.emit("tensor", lambda e, j=j: e.transpose(
                    pt_ps[:, j * 128:(j + 1) * 128],
                    pn_bf[:, j * 128:(j + 1) * 128], com.ident.ap),
                    reads=[pn_bf, com.ident], writes=[pt_ps])
            vcopy(b, pt_sb.ap, pt_ps.ap, reads=[pt_ps], writes=[pt_sb])
            for j in range(4):
                b.emit("tensor", lambda e, j=j, ti_=ti_, r0=r0, ws=wstart:
                       e.transpose(vt_ps.ap,
                                   VT[ti_][r0:r0 + 64, ws + j * 128:ws + (j + 1) * 128],
                                   com.ident[r0:r0 + 64, r0:r0 + 64],
                                   tile_position=(r0, 0)),
                       reads=[VT[ti_], com.ident], writes=[vt_ps])
                vcopy(b, vtok[:, j * 64:(j + 1) * 64], vt_ps[:, 0:64],
                      reads=[vt_ps], writes=[vtok])
            fns = []
            for j in range(4):
                fns.append(lambda e, j=j: e.matmul(
                    ps_o.ap, vtok[:, j * 64:(j + 1) * 64],
                    pt_sb[:, j * 128:(j + 1) * 128],
                    start=(j == 0), stop=(j == 3)))
            b.emit_many("tensor", fns, reads=[vtok, pt_sb], writes=[ps_o])
            b.emit("vector", lambda e, ti_=ti_, r0=r0, q0=q0: e.tensor_copy(
                aout[ti_][r0:r0 + 64, q0:q0 + 128], ps_o.ap),
                reads=[ps_o], writes=[aout[ti_]])

    for d in range(8):
        b.dma("gpsimd", wsb[d].ap, owT[d * 128:(d + 1) * 128, :], writes=[wsb[d]])
    xl = [b.sbuf(f"xrl{k}", [128, 512], F32) for k in range(2)]
    so = [b.sbuf(f"sol{k}", [128, 512], F32) for k in range(2)]

    def evac_o(oi, t0, tl, ps):
        k = (t0 // 512) % 2
        b.dma("sync", xl[k].ap, x_res[oi * 128:(oi + 1) * 128, t0:t0 + tl],
              writes=[xl[k]])
        ts(b, "vector", so[k].ap, ps[:, 0:tl], bias["o"][:, oi:oi + 1], None,
           ALU.add, reads=[ps, bias["o"]], writes=[so[k]])
        tt(b, so[k].ap, so[k].ap, xl[k].ap, ALU.add, reads=[so[k], xl[k]],
           writes=[so[k]])
        b.dma("sync", xnew[oi * 128:(oi + 1) * 128, t0:t0 + tl], so[k].ap,
              reads=[so[k]])

    emit_mm(b, lambda d, oi: (wsb[d], wsb[d][:, oi * 128:(oi + 1) * 128]), 8,
            lambda d, t0, tl: (aout[d], aout[d][:, t0:t0 + tl]),
            2048, list(range(8)), pool, evac_o)
    if ATTN_A_DEBUG:
        dq = b.out("dbg_q", [1024, 2048], BF16)
        dk = b.out("dbg_k", [1024, TE], BF16)
        dv = b.out("dbg_v", [1024, TE], BF16)
        da = b.out("dbg_a", [1024, 2048], BF16)
        dl = b.out("dbg_ln", [1024, TE], BF16)
        for i in range(8):
            b.dma("sync", dq[i * 128:(i + 1) * 128, :], QT[i].ap, reads=[QT[i]])
            b.dma("sync", dk[i * 128:(i + 1) * 128, :], KT[i].ap, reads=[KT[i]])
            b.dma("sync", dv[i * 128:(i + 1) * 128, :], VT[i].ap, reads=[VT[i]])
            b.dma("sync", da[i * 128:(i + 1) * 128, :], aout[i][:, 0:2048],
                  reads=[aout[i]])
    return b.finish()


def make_attn_a_masks(first_block_pad):
    m = np.full((4, 128, 512), -1e30, np.float32)
    for p in range(2):
        for i in range(128):
            q = p * 128 + i
            lo, hi = q, q + 256
            m[p, i, lo:hi + 1] = 0.0
            if first_block_pad:
                m[2 + p, i, max(lo, 256):hi + 1] = 0.0
            else:
                m[2 + p, i, lo:hi + 1] = 0.0
    return np.concatenate([m[0], m[1], m[2], m[3]], axis=1)


def host_attn_a(xs, wdict):
    mask0 = make_attn_a_masks(True)
    mask1 = make_attn_a_masks(False)
    in_maps = []
    for c in range(8):
        bb, h = c // 2, c % 2
        if h == 0:
            halo = np.zeros((1024, 256), np.float32)
        else:
            halo = xs[2 * bb][:, 1792:2048]
        x_ext = np.ascontiguousarray(np.concatenate([halo, xs[c]], axis=1))
        m = dict(x_ext=x_ext, x_res=xs[c], mask=(mask0 if h == 0 else mask1),
                 c_ident=IDENT_BF, **wdict)
        in_maps.append(m)
    res = _run("attn_a", build_attn_a, in_maps)
    return [np.asarray(res[c]["xnew"]) for c in range(8)]



def build_attn_b():
    b = B("attn_b").open()
    TQ, TK = 2048, 4096
    x_full = b.inp("x_full", [1024, TK], F32)
    wq = b.inp("wq", [1024, 1024], F32)
    wkv = b.inp("wkv", [1024, 2048], F32)
    wo = b.inp("wo", [1024, 1024], F32)
    bo_in = b.inp("bo", [128, 8], F32)
    scl_in = b.inp("scl", [1, 16], F32)
    sel2_in = b.inp("sel2", [2, 128], F32)
    xnew = b.out("xnew", [1024, TQ], F32)
    com = Common(b, need_ident=True)
    w = LNWork(b, "ln")

    ln_dram = b.dram("ln_dram", [1024, TK], BF16)
    q_dram = b.dram("q_dram", [1024, TQ], BF16)
    k_dram = b.dram("k_dram", [1024, TK], BF16)
    ao_dram = b.dram("ao_dram", [1024, TQ], BF16)

    bo = b.sbuf("bo_sb", [128, 8], F32)
    b.dma("sync", bo, bo_in, writes=[bo])
    scl = b.sbuf("scl_sb", [1, 16], F32)
    b.dma("sync", scl, scl_in, writes=[scl])
    sel2 = b.sbuf("sel2_sb", [2, 128], F32)
    b.dma("sync", sel2, sel2_in, writes=[sel2])
    esc_row = b.sbuf("esc_row", [1, 16], F32)
    act(b, esc_row.ap, scl.ap, AF.Exp, reads=[scl], writes=[esc_row])
    esc_bc = b.sbuf("esc_bc", [128, 16], F32)
    nesc = b.sbuf("nesc", [128, 16], F32)
    esc_hp = b.sbuf("esc_hp", [128, 8], F32)
    bcast_row_f32_t(b, com, esc_row, esc_row.ap, w.bc_b)
    vcopy(b, esc_bc.ap, w.bc_b[0:128, 0:16], reads=[w.bc_b], writes=[esc_bc])
    ts(b, "vector", nesc.ap, esc_bc.ap, -1.0, None, ALU.mult,
       reads=[esc_bc], writes=[nesc])
    for oi in range(8):
        vcopy(b, esc_hp[0:64, oi:oi + 1], esc_bc[0:64, 2 * oi:2 * oi + 1],
              reads=[esc_bc], writes=[esc_hp])
        vcopy(b, esc_hp[64:128, oi:oi + 1], esc_bc[64:128, 2 * oi + 1:2 * oi + 2],
              reads=[esc_bc], writes=[esc_hp])

    xin = [b.sbuf(f"xin{i}", [128, 512], F32) for i in range(8)]
    lnst = [b.sbuf(f"lnst{k}", [128, 512], BF16) for k in range(2)]
    lntmp = b.sbuf("lntmp", [128, 512], F32)
    for (t0, tl) in tiles(TK, 512):
        for i in range(8):
            b.dma("sync", xin[i][:, 0:tl], x_full[i * 128:(i + 1) * 128, t0:t0 + tl],
                  writes=[xin[i]])
        emit_ln_block_streamed(b, com, w, xin, lnst, lntmp, ln_dram, t0, tl)

    wsb = [b.sbuf(f"wsb{d}", [128, 1024], BF16) for d in range(8)]
    lnb = [[b.sbuf(f"lnb{s}_{i}", [128, 512], BF16) for i in range(8)]
           for s in range(2)]
    pool = [b.psum(f"mmp{k}", [128, 512], F32) for k in range(2)]
    VV = [b.sbuf(f"VV{kk}", [128, 1040], BF16) for kk in range(32)]
    for kk in range(32):
        b.emit("gpsimd", lambda e, kk=kk: e.memset(VV[kk].ap, 0.0), writes=[VV[kk]])
        for h in range(16):
            b.emit("gpsimd", lambda e, kk=kk, h=h: e.memset(
                VV[kk][:, h * 65 + 64:h * 65 + 65], 1.0),
                reads=[VV[kk]], writes=[VV[kk]])
    sqq = b.sbuf("sqq", [128, 512], BF16)
    ssp = w.bc_b
    nrm = b.sbuf("nrmq", [2, 512], F32)
    nbc = w.bc_r
    ones2 = b.sbuf("ones2", [128, 2], BF16)
    b.emit("gpsimd", lambda e: e.memset(ones2.ap, 0.0), writes=[ones2])
    b.emit("gpsimd", lambda e: e.memset(ones2[0:64, 0:1], 1.0), reads=[ones2],
           writes=[ones2])
    b.emit("gpsimd", lambda e: e.memset(ones2[64:128, 1:2], 1.0), reads=[ones2],
           writes=[ones2])
    prst = [b.sbuf(f"prst{k}", [128, 512], BF16) for k in range(2)]
    pse = b.sbuf("pse", [128, 512], F32)
    vt_ps = b.psum("vt_ps", [128, 64], BF16)

    def proj_stream(t_len, dst_dram, q_scale):
        for ci, (t0, tl) in enumerate(tiles(t_len, 512)):
            s = ci % 2
            for i in range(8):
                b.dma("sync", lnb[s][i][:, 0:tl],
                      ln_dram[:][i * 128:(i + 1) * 128, t0:t0 + tl],
                      reads=[ln_dram], writes=[lnb[s][i]])
            for oi in range(8):
                ps = pool[oi % 2]
                fns = [(lambda e, d=d, oi=oi, ps=ps, tl=tl, s=s: e.matmul(
                    ps[:, 0:tl], wsb[d][:, oi * 128:(oi + 1) * 128],
                    lnb[s][d][:, 0:tl], start=(d == 0), stop=(d == 7)))
                    for d in range(8)]
                b.emit_many("tensor", fns, reads=[*wsb, *lnb[s]], writes=[ps])
                if dst_dram is not None:
                    act(b, sqq[:, 0:tl], ps[:, 0:tl], AF.Square, reads=[ps],
                        writes=[sqq])
                    b.emit("tensor", lambda e, tl=tl: e.matmul(
                        ssp[0:2, 0:tl], ones2.ap, sqq[:, 0:tl], start=True,
                        stop=True), reads=[sqq, ones2], writes=[ssp])
                    b.emit("scalar", lambda e, tl=tl: e.activation(
                        nrm[0:2, 0:tl], ssp[0:2, 0:tl], AF.Sqrt),
                        reads=[ssp], writes=[nrm])
                    ts(b, "vector", nrm[0:2, 0:tl], nrm[0:2, 0:tl], 1e-12, None,
                       ALU.max, reads=[nrm], writes=[nrm])
                    b.emit("vector", lambda e, tl=tl: e.reciprocal(
                        nrm[0:2, 0:tl], nrm[0:2, 0:tl]), reads=[nrm],
                        writes=[nrm])
                    b.emit("tensor", lambda e, tl=tl: e.matmul(
                        nbc[0:128, 0:tl], sel2.ap, nrm[0:2, 0:tl], start=True,
                        stop=True), reads=[nrm, sel2], writes=[nbc])
                    pr = prst[oi % 2]
                    b.emit("scalar", lambda e, ps=ps, tl=tl: e.copy(
                        pse[:, 0:tl], ps[:, 0:tl]), reads=[ps], writes=[pse])
                    if q_scale:
                        stt(b, pr[:, 0:tl], pse[:, 0:tl], esc_hp[:, oi:oi + 1],
                            nbc[0:128, 0:tl], ALU.mult, ALU.mult,
                            reads=[pse, esc_hp, nbc], writes=[pr])
                    else:
                        tt(b, pr[:, 0:tl], pse[:, 0:tl], nbc[0:128, 0:tl],
                           ALU.mult, reads=[pse, nbc], writes=[pr])
                    b.dma("sync",
                          dst_dram[:][oi * 128:(oi + 1) * 128, t0:t0 + tl],
                          pr[:, 0:tl], reads=[pr], writes=[dst_dram])
                else:
                    pr = prst[oi % 2]
                    vcopy(b, pr[:, 0:tl], ps[:, 0:tl], reads=[ps], writes=[pr])
                    for j in range(tl // 128):
                        kk = (t0 + j * 128) // 128
                        for half in range(2):
                            h = 2 * oi + half
                            b.emit("tensor",
                                   lambda e, j=j, half=half: e.transpose(
                                       vt_ps.ap,
                                       pr[half * 64:(half + 1) * 64,
                                          j * 128:(j + 1) * 128],
                                       com.ident[half * 64:half * 64 + 64,
                                                 half * 64:half * 64 + 64],
                                       tile_position=(half * 64, 0)),
                                   reads=[pr, com.ident], writes=[vt_ps])
                            vcopy(b, VV[kk][:, h * 65:h * 65 + 64],
                                  vt_ps[:, 0:64], reads=[vt_ps],
                                  writes=[VV[kk]])

    for d in range(8):
        b.dma("gpsimd", wsb[d].ap, wq[d * 128:(d + 1) * 128, :], writes=[wsb[d]])
    proj_stream(TQ, q_dram, True)
    for d in range(8):
        b.dma("gpsimd", wsb[d].ap, wkv[d * 128:(d + 1) * 128, 0:1024],
              writes=[wsb[d]])
    proj_stream(TK, k_dram, False)
    for d in range(8):
        b.dma("gpsimd", wsb[d].ap, wkv[d * 128:(d + 1) * 128, 1024:2048],
              writes=[wsb[d]])
    proj_stream(TK, None, False)

    kh = [b.sbuf(f"kh{s}", [64, TK], BF16) for s in range(2)]
    qh = [b.sbuf(f"qh{s}", [64, 512], BF16) for s in range(2)]
    st_ps = [b.psum(f"st_ps{k}", [128, 512], F32) for k in range(2)]
    ot_ps = b.psum("ot_ps", [65, 512], F32)
    pt_bf = [b.sbuf(f"pt_bf{k}", [128, 512], BF16) for k in range(2)]
    den_sb = b.sbuf("den_sb", [128, 512], F32)
    den0 = b.sbuf("den0", [1, 512], F32)
    ot_sb = b.sbuf("ot_sb", [64, 512], F32)
    ao_st = b.sbuf("ao_st", [64, 512], BF16)
    for h in range(16):
        r = (h // 2) * 128 + (h % 2) * 64
        b.dma("sync", kh[h % 2].ap, k_dram[:][r:r + 64, :],
              reads=[k_dram], writes=[kh[h % 2]])
        for qb in range(4):
            q0 = qb * 512
            b.dma("sync", qh[qb % 2].ap, q_dram[:][r:r + 64, q0:q0 + 512],
                  reads=[q_dram], writes=[qh[qb % 2]])

            def emit_st(kt):
                b.emit("tensor", lambda e, kt=kt: e.matmul(
                    st_ps[kt % 2].ap, kh[h % 2][:, kt * 128:(kt + 1) * 128],
                    qh[qb % 2].ap, start=True, stop=True),
                    reads=[kh[h % 2], qh[qb % 2]], writes=[st_ps[kt % 2]])
                act(b, pt_bf[kt % 2].ap, st_ps[kt % 2].ap, AF.Exp,
                    bias=nesc[0:128, h:h + 1], scale=1.0,
                    reads=[st_ps[kt % 2], nesc], writes=[pt_bf[kt % 2]])

            def emit_pv(kt):
                b.emit("tensor", lambda e, kt=kt: e.matmul(
                    ot_ps.ap, VV[kt][:, h * 65:h * 65 + 65], pt_bf[kt % 2].ap,
                    start=(kt == 0), stop=(kt == 31)),
                    reads=[VV[kt], pt_bf[kt % 2]], writes=[ot_ps])

            emit_st(0)
            for kt in range(1, 32):
                emit_st(kt)
                emit_pv(kt - 1)
            emit_pv(31)
            b.emit("scalar", lambda e: e.copy(den_sb[64:65, :], ot_ps[64:65, :]),
                   reads=[ot_ps], writes=[den_sb])
            b.dma("sync", den0.ap, den_sb[64:65, :], reads=[den_sb], writes=[den0])
            b.emit("vector", lambda e: e.reciprocal(den0.ap, den0.ap),
                   reads=[den0], writes=[den0])
            b.emit("tensor", lambda e: e.matmul(st_ps[0][0:64, :],
                                                com.ones_row_f32[:, 0:64],
                                                den0.ap, start=True, stop=True),
                   reads=[den0, com.ones_row_f32], writes=[st_ps[0]])
            b.emit("scalar", lambda e: e.copy(ot_sb.ap, ot_ps[0:64, :]),
                   reads=[ot_ps], writes=[ot_sb])
            tt(b, ao_st.ap, ot_sb.ap, st_ps[0][0:64, :], ALU.mult,
               reads=[ot_sb, st_ps[0]], writes=[ao_st])
            b.dma("sync", ao_dram[:][r:r + 64, q0:q0 + 512], ao_st.ap,
                  reads=[ao_st], writes=[ao_dram])

    for d in range(8):
        b.dma("gpsimd", wsb[d].ap, wo[d * 128:(d + 1) * 128, :], writes=[wsb[d]])
    xl = [b.sbuf(f"xrl{k}", [128, 512], F32) for k in range(2)]
    so = [b.sbuf(f"sol{k}", [128, 512], F32) for k in range(2)]
    for ci, (t0, tl) in enumerate(tiles(TQ, 512)):
        s = ci % 2
        for i in range(8):
            b.dma("sync", lnb[s][i][:, 0:tl],
                  ao_dram[:][i * 128:(i + 1) * 128, t0:t0 + tl],
                  reads=[ao_dram], writes=[lnb[s][i]])
        for oi in range(8):
            ps = pool[oi % 2]
            fns = [(lambda e, d=d, oi=oi, ps=ps, tl=tl, s=s: e.matmul(
                ps[:, 0:tl], wsb[d][:, oi * 128:(oi + 1) * 128],
                lnb[s][d][:, 0:tl], start=(d == 0), stop=(d == 7)))
                for d in range(8)]
            b.emit_many("tensor", fns, reads=[*wsb, *lnb[s]], writes=[ps])
            k = oi % 2
            b.dma("sync", xl[k].ap, x_full[oi * 128:(oi + 1) * 128, t0:t0 + tl],
                  writes=[xl[k]])
            ts(b, "vector", so[k][:, 0:tl], ps[:, 0:tl], bo[:, oi:oi + 1], None,
               ALU.add, reads=[ps, bo], writes=[so[k]])
            tt(b, so[k][:, 0:tl], so[k][:, 0:tl], xl[k][:, 0:tl], ALU.add,
               reads=[so[k], xl[k]], writes=[so[k]])
            b.dma("sync", xnew[oi * 128:(oi + 1) * 128, t0:t0 + tl],
                  so[k][:, 0:tl], reads=[so[k]])
    return b.finish()


def emit_ln_block_streamed(b, com, w, x_tiles, stages, lntmp, out_dram, t0, tl):
    nf = len(x_tiles)
    d = nf * 128
    for i in range(nf):
        vcopy(b, w.xbf[:, 0:tl], x_tiles[i][:, 0:tl], reads=[x_tiles[i]],
              writes=[w.xbf])
        act(b, w.x2bf[:, 0:tl], x_tiles[i][:, 0:tl], AF.Square,
            reads=[x_tiles[i]], writes=[w.x2bf])
        b.emit("tensor", lambda e, i=i: e.matmul(w.ps_s[0:1, 0:tl],
                                                 com.ones_col_bf.ap,
                                                 w.xbf[:, 0:tl], start=(i == 0),
                                                 stop=(i == nf - 1)),
               reads=[w.xbf, com.ones_col_bf], writes=[w.ps_s])
        b.emit("tensor", lambda e, i=i: e.matmul(w.ps_q[0:1, 0:tl],
                                                 com.ones_col_bf.ap,
                                                 w.x2bf[:, 0:tl], start=(i == 0),
                                                 stop=(i == nf - 1)),
               reads=[w.x2bf, com.ones_col_bf], writes=[w.ps_q])
    ts(b, "vector", w.row_mu[:, 0:tl], w.ps_s[0:1, 0:tl], 1.0 / d, None, ALU.mult,
       reads=[w.ps_s], writes=[w.row_mu])
    stt(b, w.row_b[:, 0:tl], w.row_mu[:, 0:tl], -1.0, w.row_mu[:, 0:tl],
        ALU.mult, ALU.mult, reads=[w.row_mu], writes=[w.row_b])
    b.emit("vector", lambda e: e.scalar_tensor_tensor(
        w.row_r[:, 0:tl], w.ps_q[0:1, 0:tl], 1.0 / d, w.row_b[:, 0:tl],
        ALU.mult, ALU.add), reads=[w.ps_q, w.row_b], writes=[w.row_r])
    act(b, w.row_r[:, 0:tl], w.row_r[:, 0:tl], AF.Ln,
        bias=com.const_col(1e-5, rows=1), scale=1.0, reads=[w.row_r],
        writes=[w.row_r])
    act(b, w.row_r[:, 0:tl], w.row_r[:, 0:tl], AF.Exp, bias=0.0, scale=-0.5,
        reads=[w.row_r], writes=[w.row_r])
    stt(b, w.row_b[:, 0:tl], w.row_mu[:, 0:tl], -1.0, w.row_r[:, 0:tl],
        ALU.mult, ALU.mult, reads=[w.row_mu, w.row_r], writes=[w.row_b])
    bcast_row_f32_t(b, com, w.row_r, w.row_r[:, 0:tl], w.bc_r)
    bcast_row_f32_t(b, com, w.row_b, w.row_b[:, 0:tl], w.bc_b)
    for i in range(nf):
        sg = stages[i % 2]
        tt(b, lntmp[:, 0:tl], x_tiles[i][:, 0:tl], w.bc_r[0:128, 0:tl], ALU.mult,
           reads=[x_tiles[i], w.bc_r], writes=[lntmp])
        tt(b, sg[:, 0:tl], lntmp[:, 0:tl], w.bc_b[0:128, 0:tl], ALU.add,
           reads=[lntmp, w.bc_b], writes=[sg])
        b.dma("sync", out_dram[:][i * 128:(i + 1) * 128, t0:t0 + tl], sg[:, 0:tl],
              reads=[sg], writes=[out_dram])


SEL2_F32 = np.zeros((2, 128), np.float32)
SEL2_F32[0, 0:64] = 1.0
SEL2_F32[1, 64:128] = 1.0


def host_attn_b(xs, wdict):
    in_maps = []
    for c in range(8):
        partner = c ^ 1
        x_full = np.ascontiguousarray(np.concatenate([xs[c], xs[partner]], axis=1))
        in_maps.append(dict(x_full=x_full, c_ident=IDENT_BF, sel2=SEL2_F32, **wdict))
    res = _run("attn_b", build_attn_b, in_maps)
    return [np.asarray(res[c]["xnew"]) for c in range(8)]



SSMA_DEBUG = False


def build_ssma_A():
    b = B("ssma_A").open()
    T = 4096
    x_full = b.inp("x_full", [1024, T], F32)
    winT_xi = b.inp("winT_xi", [1024, 1024], F32)
    winT_z = b.inp("winT_z", [1024, 1024], F32)
    wdtT = b.inp("wdtT", [1024, 1024], F32)
    wbcT = b.inp("wbcT", [1024, 32], F32)
    alog_in = b.inp("alog", [128, 128], F32)
    bdt_in = b.inp("bdt", [128, 8], F32)
    dtb_in = b.inp("dtb", [128, 8], F32)
    out_fm = b.out("out_fm", [1024, T], BF16)
    mom_s = b.out("mom_s", [1, T], F32)
    mom_q = b.out("mom_q", [1, T], F32)
    com = Common(b)
    w = LNWork(b, "ln")

    ln_dram = b.dram("ln_dram", [1024, T], BF16)
    xi_dram = b.dram("xi_dram", [1024, T], BF16)
    z_dram = b.dram("z_dram", [1024, T], BF16)
    dt_dram = b.dram("dt_dram", [1024, T], BF16)

    alog = b.sbuf("alog_sb", [128, 128], F32)
    b.dma("sync", alog, alog_in, writes=[alog])
    asc = b.sbuf("asc", [128, 128], F32)
    act(b, asc.ap, alog.ap, AF.Exp, bias=com.const_col(float(np.log(0.5))),
        scale=1.0, reads=[alog, com.cpool], writes=[asc])
    bias8 = b.sbuf("bias8", [128, 8], F32)
    dtb8 = b.sbuf("dtb8", [128, 8], F32)
    b.dma("sync", bias8, bdt_in, writes=[bias8])
    b.dma("sync", dtb8, dtb_in, writes=[dtb8])
    tt(b, bias8.ap, bias8.ap, dtb8.ap, ALU.add, reads=[bias8, dtb8], writes=[bias8])

    xin = [b.sbuf(f"xin{i}", [128, 512], F32) for i in range(8)]
    lnst = [b.sbuf(f"lnst{k}", [128, 512], BF16) for k in range(2)]
    lntmp = b.sbuf("lntmp", [128, 512], F32)
    for (t0, tl) in tiles(T, 512):
        for i in range(8):
            b.dma("sync", xin[i][:, 0:tl], x_full[i * 128:(i + 1) * 128, t0:t0 + tl],
                  writes=[xin[i]])
        emit_ln_block_streamed(b, com, w, xin, lnst, lntmp, ln_dram, t0, tl)

    wsb = [b.sbuf(f"wsb{d}", [128, 1024], BF16) for d in range(8)]
    lnb = [[b.sbuf(f"lnb{s}_{i}", [128, 512], BF16) for i in range(8)]
           for s in range(2)]
    pool = [b.psum(f"mmp{k}", [128, 512], F32) for k in range(2)]
    prst = [b.sbuf(f"prst{k}", [128, 512], BF16) for k in range(2)]
    spt = b.sbuf("spt", [128, 512], F32)
    bmc = b.sbuf("bmc", [32, T], BF16)

    def proj(dst_dram, softplus, bmc_mode):
        for ci, (t0, tl) in enumerate(tiles(T, 512)):
            s = ci % 2
            for i in range(8):
                b.dma("sync", lnb[s][i][:, 0:tl],
                      ln_dram[:][i * 128:(i + 1) * 128, t0:t0 + tl],
                      reads=[ln_dram], writes=[lnb[s][i]])
            o_list = [0] if bmc_mode else list(range(8))
            for oi in o_list:
                ps = pool[oi % 2]
                ocols = 32 if bmc_mode else 128
                fns = [(lambda e, d=d, oi=oi, ps=ps, tl=tl, s=s, oc=ocols:
                        e.matmul(ps[0:oc, 0:tl],
                                 wsb[d][:, oi * oc:(oi + 1) * oc],
                                 lnb[s][d][:, 0:tl], start=(d == 0),
                                 stop=(d == 7))) for d in range(8)]
                b.emit_many("tensor", fns, reads=[*wsb, *lnb[s]], writes=[ps])
                pr = prst[oi % 2]
                if bmc_mode:
                    vcopy(b, bmc[:, t0:t0 + tl], ps[0:32, 0:tl], reads=[ps],
                          writes=[bmc])
                elif softplus:
                    ts(b, "vector", spt[:, 0:tl], ps[:, 0:tl],
                       bias8[:, oi:oi + 1], None, ALU.add,
                       reads=[ps, bias8], writes=[spt])
                    act(b, spt[:, 0:tl], spt[:, 0:tl], AF.Exp, reads=[spt],
                        writes=[spt])
                    act(b, pr[:, 0:tl], spt[:, 0:tl], AF.Ln, bias=1.0, scale=1.0,
                        reads=[spt], writes=[pr])
                    b.dma("sync", dst_dram[:][oi * 128:(oi + 1) * 128, t0:t0 + tl],
                          pr[:, 0:tl], reads=[pr], writes=[dst_dram])
                else:
                    vcopy(b, pr[:, 0:tl], ps[:, 0:tl], reads=[ps], writes=[pr])
                    b.dma("sync", dst_dram[:][oi * 128:(oi + 1) * 128, t0:t0 + tl],
                          pr[:, 0:tl], reads=[pr], writes=[dst_dram])

    for d in range(8):
        b.dma("gpsimd", wsb[d].ap, winT_xi[d * 128:(d + 1) * 128, :], writes=[wsb[d]])
    proj(xi_dram, False, False)
    for d in range(8):
        b.dma("gpsimd", wsb[d].ap, winT_z[d * 128:(d + 1) * 128, :], writes=[wsb[d]])
    proj(z_dram, False, False)
    for d in range(8):
        b.dma("gpsimd", wsb[d].ap, wdtT[d * 128:(d + 1) * 128, :], writes=[wsb[d]])
    proj(dt_dram, True, False)
    for d in range(8):
        b.dma("gpsimd", wsb[d][:, 0:32], wbcT[d * 128:(d + 1) * 128, :],
              writes=[wsb[d]])
    proj(None, False, True)

    TB = 1024
    xib = [b.sbuf(f"xib{i}", [128, TB], BF16) for i in range(8)]
    dtb_ = [b.sbuf(f"dtb{i}", [128, TB], BF16) for i in range(8)]
    v2 = [b.sbuf(f"v2_{i}", [128, TB], BF16) for i in range(8)]
    yacc = [b.sbuf(f"yacc{i}", [128, TB], F32) for i in range(8)]
    carry = [b.sbuf(f"carry{i}", [128, 16], F32) for i in range(8)]
    lnw = b.sbuf("lnw", [128, TB], F32)
    rres = b.sbuf("rres", [128, TB], F32)
    dnm = b.sbuf("dnm", [128, TB], F32)
    nwm = b.sbuf("nwm", [128, TB], F32)
    a_t = b.sbuf("a_t", [128, TB], F32)
    u_t = b.sbuf("u_t", [128, TB], F32)
    h_t = b.sbuf("h_t", [128, TB], F32)
    brow = b.sbuf("brow", [1, TB], BF16)
    crow = b.sbuf("crow", [1, TB], BF16)
    bm_bc = b.psum("bm_bc", [128, TB], F32)
    c_bc = b.psum("c_bc", [128, TB], F32)
    for i in range(8):
        b.emit("gpsimd", lambda e, i=i: e.memset(carry[i].ap, 0.0),
               writes=[carry[i]])
    zs = b.sbuf("zs", [128, TB], F32)
    outb = [b.sbuf(f"outb{k}", [128, TB], BF16) for k in range(2)]
    sqb = b.sbuf("sqb", [128, TB], BF16)
    momst = [b.sbuf(f"momst{k}", [1, TB], F32) for k in range(2)]

    for ti, (t0, tl) in enumerate(tiles(T, TB)):
        for i in range(8):
            b.dma("sync", xib[i].ap, xi_dram[:][i * 128:(i + 1) * 128, t0:t0 + tl],
                  reads=[xi_dram], writes=[xib[i]])
            b.dma("sync", dtb_[i].ap, dt_dram[:][i * 128:(i + 1) * 128, t0:t0 + tl],
                  reads=[dt_dram], writes=[dtb_[i]])
            tt(b, v2[i].ap, dtb_[i].ap, xib[i].ap, ALU.mult,
               reads=[dtb_[i], xib[i]], writes=[v2[i]])
        for s in range(16):
            b.dma("sync", brow.ap, bmc[s:s + 1, t0:t0 + tl], reads=[bmc],
                  writes=[brow])
            b.dma("sync", crow.ap, bmc[16 + s:17 + s, t0:t0 + tl], reads=[bmc],
                  writes=[crow])
            bcast_row_bf(b, com, brow, brow.ap, bm_bc)
            bcast_row_bf(b, com, crow, crow.ap, c_bc)
            for d in range(8):
                col = d * 16 + s
                ts(b, "vector", dnm.ap, dtb_[d].ap, asc[0:128, col:col + 1],
                   1.0, ALU.mult, ALU.add, reads=[dtb_[d], asc], writes=[dnm])
                act(b, lnw.ap, dtb_[d].ap, AF.Ln, bias=1.0,
                    scale=asc[0:128, col:col + 1], reads=[dtb_[d], asc],
                    writes=[lnw])
                act(b, rres.ap, lnw.ap, AF.Exp, bias=0.0, scale=-1.0,
                    reads=[lnw], writes=[rres])
                tt(b, nwm.ap, dnm.ap, rres.ap, ALU.mult, reads=[dnm, rres],
                   writes=[nwm])
                ts(b, "vector", nwm.ap, nwm.ap, -1.0, 2.0, ALU.mult, ALU.add,
                   reads=[nwm], writes=[nwm])
                tt(b, rres.ap, rres.ap, nwm.ap, ALU.mult, reads=[rres, nwm],
                   writes=[rres])
                ts(b, "vector", a_t.ap, rres.ap, 2.0, -1.0, ALU.mult, ALU.add,
                   reads=[rres], writes=[a_t])
                tt(b, u_t.ap, v2[d].ap, rres.ap, ALU.mult,
                   reads=[v2[d], rres], writes=[u_t])
                tt(b, u_t.ap, u_t.ap, bm_bc[0:128, 0:tl], ALU.mult,
                   reads=[u_t, bm_bc], writes=[u_t])
                b.emit("vector", lambda e, d=d, s=s: e.tensor_tensor_scan(
                    h_t.ap, a_t.ap, u_t.ap, carry[d][:, s:s + 1],
                    ALU.mult, ALU.add),
                    reads=[a_t, u_t, carry[d]], writes=[h_t])
                vcopy(b, carry[d][:, s:s + 1], h_t[:, tl - 1:tl],
                      reads=[h_t, carry[d]], writes=[carry[d]])
                if s == 0:
                    tt(b, yacc[d].ap, h_t.ap, c_bc[0:128, 0:tl], ALU.mult,
                       reads=[h_t, c_bc], writes=[yacc[d]])
                else:
                    tt(b, nwm.ap, h_t.ap, c_bc[0:128, 0:tl], ALU.mult,
                       reads=[h_t, c_bc], writes=[nwm])
                    tt(b, yacc[d].ap, yacc[d].ap, nwm.ap, ALU.add,
                       reads=[yacc[d], nwm], writes=[yacc[d]])
        for i in range(8):
            b.dma("sync", xib[i].ap, z_dram[:][i * 128:(i + 1) * 128, t0:t0 + tl],
                  reads=[z_dram], writes=[xib[i]])
        for i in range(8):
            ob = outb[i % 2]
            act(b, zs.ap, xib[i].ap, AF.Silu, reads=[xib[i]], writes=[zs])
            tt(b, ob.ap, yacc[i].ap, zs.ap, ALU.mult, reads=[yacc[i], zs],
               writes=[ob])
            b.dma("sync", out_fm[i * 128:(i + 1) * 128, t0:t0 + tl], ob.ap,
                  reads=[ob])
            act(b, sqb.ap, ob.ap, AF.Square, reads=[ob], writes=[sqb])
            for hf in range(2):
                b.emit("tensor", lambda e, i=i, hf=hf: e.matmul(
                    bm_bc[0:1, hf * 512:hf * 512 + 512], com.ones_col_bf.ap,
                    ob[:, hf * 512:hf * 512 + 512],
                    start=(i == 0), stop=(i == 7)),
                    reads=[ob, com.ones_col_bf], writes=[bm_bc])
                b.emit("tensor", lambda e, i=i, hf=hf: e.matmul(
                    c_bc[0:1, hf * 512:hf * 512 + 512], com.ones_col_bf.ap,
                    sqb[:, hf * 512:hf * 512 + 512],
                    start=(i == 0), stop=(i == 7)),
                    reads=[sqb, com.ones_col_bf], writes=[c_bc])
        vcopy(b, momst[0].ap, bm_bc[0:1, 0:tl], reads=[bm_bc],
              writes=[momst[0]])
        vcopy(b, momst[1].ap, c_bc[0:1, 0:tl], reads=[c_bc],
              writes=[momst[1]])
        b.dma("sync", mom_s[0:1, t0:t0 + tl], momst[0].ap, reads=[momst[0]])
        b.dma("sync", mom_q[0:1, t0:t0 + tl], momst[1].ap, reads=[momst[1]])
    if SSMA_DEBUG:
        d_xi = b.out("dbg_xi", [1024, T], BF16)
        d_dt = b.out("dbg_dt", [1024, T], BF16)
        d_z = b.out("dbg_z", [1024, T], BF16)
        d_bmc = b.out("dbg_bmc", [32, T], BF16)
        big = b.sbuf("dbg_big", [128, 1024], BF16)
        for i in range(8):
            for src_d, dst in [(xi_dram, d_xi), (dt_dram, d_dt), (z_dram, d_z)]:
                for hh in range(4):
                    b.dma("sync", big.ap, src_d[:][i * 128:(i + 1) * 128, hh * 1024:(hh + 1) * 1024],
                          reads=[src_d], writes=[big])
                    b.dma("sync", dst[i * 128:(i + 1) * 128, hh * 1024:(hh + 1) * 1024], big.ap, reads=[big])
        b.dma("sync", d_bmc, bmc.ap, reads=[bmc])
    return b.finish()


def build_ssma_B():
    b = B("ssma_B").open()
    TQ = 2048
    outf = b.inp("outf", [2048, TQ], BF16)
    ms0 = b.inp("ms0", [1, TQ], F32)
    ms1 = b.inp("ms1", [1, TQ], F32)
    mq0 = b.inp("mq0", [1, TQ], F32)
    mq1 = b.inp("mq1", [1, TQ], F32)
    gam = b.inp("gam", [128, 16], F32)
    bet = b.inp("bet", [128, 16], F32)
    woutT = b.inp("woutT", [2048, 1024], F32)
    x_res = b.inp("x_res", [1024, TQ], F32)
    xnew = b.out("xnew", [1024, TQ], F32)
    com = Common(b)
    DI = 2048

    gam_sb = b.sbuf("gam_sb", [128, 16], F32)
    bet_sb = b.sbuf("bet_sb", [128, 16], F32)
    b.dma("sync", gam_sb, gam, writes=[gam_sb])
    b.dma("sync", bet_sb, bet, writes=[bet_sb])
    rows = b.sbuf("rows", [1, TQ], F32)
    rowq = b.sbuf("rowq", [1, TQ], F32)
    rowb = b.sbuf("rowb", [1, TQ], F32)
    m8 = b.sbuf("m8", [1, TQ], F32)
    b.dma("sync", rows, ms0, writes=[rows])
    b.dma("sync", m8, ms1, writes=[m8])
    tt(b, rows.ap, rows.ap, m8.ap, ALU.add, reads=[rows, m8], writes=[rows])
    ts(b, "vector", rows.ap, rows.ap, 1.0 / DI, None, ALU.mult, reads=[rows],
       writes=[rows])
    b.dma("sync", rowq, mq0, writes=[rowq])
    b.dma("sync", m8, mq1, writes=[m8])
    tt(b, rowq.ap, rowq.ap, m8.ap, ALU.add, reads=[rowq, m8], writes=[rowq])
    stt(b, m8.ap, rows.ap, -1.0, rows.ap, ALU.mult, ALU.mult,
        reads=[rows], writes=[m8])
    stt(b, rowq.ap, rowq.ap, 1.0 / DI, m8.ap, ALU.mult, ALU.add,
        reads=[rowq, m8], writes=[rowq])
    act(b, rowq.ap, rowq.ap, AF.Ln, bias=com.const_col(1e-5, rows=1), scale=1.0,
        reads=[rowq, com.cpool], writes=[rowq])
    act(b, rowq.ap, rowq.ap, AF.Exp, bias=0.0, scale=-0.5, reads=[rowq],
        writes=[rowq])
    stt(b, rowb.ap, rows.ap, -1.0, rowq.ap, ALU.mult, ALU.mult,
        reads=[rows, rowq], writes=[rowb])

    wsb = [b.sbuf(f"wsb{d}", [128, 1024], BF16) for d in range(16)]
    for d in range(16):
        b.dma("gpsimd", wsb[d].ap, woutT[d * 128:(d + 1) * 128, :], writes=[wsb[d]])
    pool = [b.psum(f"mmp{k}", [128, 512], F32) for k in range(2)]
    bc_r = b.psum("bc_r", [128, 512], F32)
    bc_b = b.psum("bc_b", [128, 512], F32)
    otile = [b.sbuf(f"ot{i}", [128, 512], BF16) for i in range(16)]
    nrm = [b.sbuf(f"nt{i}", [128, 512], BF16) for i in range(16)]
    ntmp = b.sbuf("ntmp", [128, 512], F32)
    xl = [b.sbuf(f"xrl{k}", [128, 512], F32) for k in range(2)]
    so = [b.sbuf(f"sol{k}", [128, 512], F32) for k in range(2)]
    for ci, (t0, tl) in enumerate(tiles(TQ, 512)):
        bcast_row_f32_t(b, com, rowq, rowq[0:1, t0:t0 + tl], bc_r)
        bcast_row_f32_t(b, com, rowb, rowb[0:1, t0:t0 + tl], bc_b)
        for i in range(16):
            b.dma("sync", otile[i][:, 0:tl],
                  outf[i * 128:(i + 1) * 128, t0:t0 + tl], writes=[otile[i]])
            tt(b, ntmp[:, 0:tl], otile[i][:, 0:tl], bc_r[0:128, 0:tl], ALU.mult,
               reads=[otile[i], bc_r], writes=[ntmp])
            tt(b, ntmp[:, 0:tl], ntmp[:, 0:tl], bc_b[0:128, 0:tl], ALU.add,
               reads=[ntmp, bc_b], writes=[ntmp])
            ts(b, "vector", nrm[i][:, 0:tl], ntmp[:, 0:tl], gam_sb[:, i:i + 1],
               bet_sb[:, i:i + 1], ALU.mult, ALU.add,
               reads=[ntmp, gam_sb, bet_sb], writes=[nrm[i]])
        for oi in range(8):
            ps = pool[oi % 2]
            fns = [(lambda e, d=d, oi=oi, ps=ps, tl=tl: e.matmul(
                ps[:, 0:tl], wsb[d][:, oi * 128:(oi + 1) * 128],
                nrm[d][:, 0:tl], start=(d == 0), stop=(d == 15)))
                for d in range(16)]
            b.emit_many("tensor", fns, reads=[*wsb, *nrm], writes=[ps])
            k = oi % 2
            b.dma("sync", xl[k][:, 0:tl], x_res[oi * 128:(oi + 1) * 128, t0:t0 + tl],
                  writes=[xl[k]])
            tt(b, so[k][:, 0:tl], ps[:, 0:tl], xl[k][:, 0:tl], ALU.add,
               reads=[ps, xl[k]], writes=[so[k]])
            b.dma("sync", xnew[oi * 128:(oi + 1) * 128, t0:t0 + tl],
                  so[k][:, 0:tl], reads=[so[k]])
    return b.finish()


def host_ssma(xs, wd):
    in_maps = []
    for c in range(8):
        bb, h = c // 2, c % 2
        x_full = np.ascontiguousarray(
            np.concatenate([xs[2 * bb], xs[2 * bb + 1]], axis=1))
        in_maps.append(dict(
            x_full=x_full,
            winT_xi=wd["winT"][:, h * 1024:(h + 1) * 1024],
            winT_z=wd["winT"][:, 2048 + h * 1024:2048 + (h + 1) * 1024],
            wdtT=wd["wdtT"][:, h * 1024:(h + 1) * 1024],
            wbcT=wd["wbcT"],
            alog=wd["alog"][h], bdt=wd["bdt"][h], dtb=wd["dtb"][h]))
    ra = _run("ssma_A", build_ssma_A, in_maps)
    in_maps2 = []
    for c in range(8):
        bb, h = c // 2, c % 2
        o0 = np.asarray(ra[2 * bb]["out_fm"])[:, h * 2048:(h + 1) * 2048]
        o1 = np.asarray(ra[2 * bb + 1]["out_fm"])[:, h * 2048:(h + 1) * 2048]
        outf = np.ascontiguousarray(np.concatenate([o0, o1], axis=0))
        sl = np.s_[:, h * 2048:(h + 1) * 2048]
        in_maps2.append(dict(
            outf=outf,
            ms0=np.ascontiguousarray(np.asarray(ra[2 * bb]["mom_s"])[sl]),
            ms1=np.ascontiguousarray(np.asarray(ra[2 * bb + 1]["mom_s"])[sl]),
            mq0=np.ascontiguousarray(np.asarray(ra[2 * bb]["mom_q"])[sl]),
            mq1=np.ascontiguousarray(np.asarray(ra[2 * bb + 1]["mom_q"])[sl]),
            gam=wd["gam"], bet=wd["bet"], woutT=wd["woutT"], x_res=xs[c]))
    rb = _run("ssma_B", build_ssma_B, in_maps2)
    return [np.asarray(rb[c]["xnew"]) for c in range(8)]



def build_ssmb_A():
    b = B("ssmb_A").open()
    T = 4096
    x_full = b.inp("x_full", [1024, T], F32)
    winT_xs = b.inp("winT_xs", [1024, 1024], F32)
    winT_z = b.inp("winT_z", [1024, 1024], F32)
    convw = b.inp("convw", [128, 32], F32)
    convb = b.inp("convb", [128, 8], F32)
    wxT = b.inp("wxT", [1024, 96], F32)
    xc_out = b.out("xc", [1024, T], BF16)
    z_out = b.out("z", [1024, T], BF16)
    xdbl_out = b.out("xdbl", [96, T], F32)
    com = Common(b)
    w = LNWork(b, "ln")

    ln_dram = b.dram("ln_dram", [1024, T], BF16)
    cw = b.sbuf("cw_sb", [128, 32], F32)
    cb = b.sbuf("cb_sb", [128, 8], F32)
    b.dma("sync", cw, convw, writes=[cw])
    b.dma("sync", cb, convb, writes=[cb])

    xin = [b.sbuf(f"xin{i}", [128, 512], F32) for i in range(8)]
    lnst = [b.sbuf(f"lnst{k}", [128, 512], BF16) for k in range(2)]
    lntmp = b.sbuf("lntmp", [128, 512], F32)
    for (t0, tl) in tiles(T, 512):
        for i in range(8):
            b.dma("sync", xin[i][:, 0:tl], x_full[i * 128:(i + 1) * 128, t0:t0 + tl],
                  writes=[xin[i]])
        emit_ln_block_streamed(b, com, w, xin, lnst, lntmp, ln_dram, t0, tl)

    wsb = [b.sbuf(f"wsb{d}", [128, 1024], BF16) for d in range(8)]
    lnb = [[b.sbuf(f"lnb{s}_{i}", [128, 512], BF16) for i in range(8)]
           for s in range(2)]
    pool = [b.psum(f"mmp{k}", [128, 512], F32) for k in range(2)]
    prst = [b.sbuf(f"prst{k}", [128, 512], BF16) for k in range(2)]

    def proj(dst_dram):
        for ci, (t0, tl) in enumerate(tiles(T, 512)):
            s = ci % 2
            for i in range(8):
                b.dma("sync", lnb[s][i][:, 0:tl],
                      ln_dram[:][i * 128:(i + 1) * 128, t0:t0 + tl],
                      reads=[ln_dram], writes=[lnb[s][i]])
            for oi in range(8):
                ps = pool[oi % 2]
                fns = [(lambda e, d=d, oi=oi, ps=ps, tl=tl, s=s: e.matmul(
                    ps[:, 0:tl], wsb[d][:, oi * 128:(oi + 1) * 128],
                    lnb[s][d][:, 0:tl], start=(d == 0), stop=(d == 7)))
                    for d in range(8)]
                b.emit_many("tensor", fns, reads=[*wsb, *lnb[s]], writes=[ps])
                pr = prst[oi % 2]
                vcopy(b, pr[:, 0:tl], ps[:, 0:tl], reads=[ps], writes=[pr])
                b.dma("sync", dst_dram[:][oi * 128:(oi + 1) * 128, t0:t0 + tl],
                      pr[:, 0:tl], reads=[pr], writes=[dst_dram])

    for d in range(8):
        b.dma("gpsimd", wsb[d].ap, winT_z[d * 128:(d + 1) * 128, :], writes=[wsb[d]])
    zd = b.dram("z_dram", [1024, T], BF16)
    proj(zd)
    zst = b.sbuf("zst", [128, 2048], BF16)
    for i in range(8):
        for hh in range(2):
            b.dma("sync", zst.ap, zd[:][i * 128:(i + 1) * 128,
                                        hh * 2048:(hh + 1) * 2048],
                  reads=[zd], writes=[zst])
            b.dma("sync", z_out[i * 128:(i + 1) * 128, hh * 2048:(hh + 1) * 2048],
                  zst.ap, reads=[zst])
    xsd = b.dram("xs_dram", [1024, T], BF16)
    for d in range(8):
        b.dma("gpsimd", wsb[d].ap, winT_xs[d * 128:(d + 1) * 128, :],
              writes=[wsb[d]])
    proj(xsd)

    wx = b.sbuf("wx_sb", [128, 768], BF16)
    for d in range(8):
        b.dma("gpsimd", wx[:, d * 96:(d + 1) * 96],
              wxT[d * 128:(d + 1) * 128, :], reads=[wx], writes=[wx])
    TB = 512
    xsp = [b.sbuf(f"xsp{i}", [128, TB + 3], BF16) for i in range(8)]
    acc = b.sbuf("acc", [128, TB], F32)
    xcb = [b.sbuf(f"xcb{i}", [128, TB], BF16) for i in range(8)]
    xdbl_sb = b.sbuf("xdbl_sb", [96, T], F32)
    ps_x = b.psum("ps_x", [96, TB], F32)
    for ci, (t0, tl) in enumerate(tiles(T, TB)):
        for i in range(8):
            if ci == 0:
                b.emit("gpsimd", lambda e, i=i: e.memset(xsp[i][:, 0:3], 0.0),
                       writes=[xsp[i]])
            else:
                b.dma("sync", xsp[i][:, 0:3],
                      xsd[:][i * 128:(i + 1) * 128, t0 - 3:t0],
                      reads=[xsd], writes=[xsp[i]])
            b.dma("sync", xsp[i][:, 3:3 + tl],
                  xsd[:][i * 128:(i + 1) * 128, t0:t0 + tl],
                  reads=[xsd], writes=[xsp[i]])
            ts(b, "vector", acc.ap, xsp[i][:, 0:tl], cw[:, i * 4:i * 4 + 1],
               None, ALU.mult, reads=[xsp[i], cw], writes=[acc])
            for k in range(1, 4):
                stt(b, acc.ap, xsp[i][:, k:k + tl], cw[:, i * 4 + k:i * 4 + k + 1],
                    acc.ap, ALU.mult, ALU.add, reads=[xsp[i], cw, acc],
                    writes=[acc])
            act(b, xcb[i].ap, acc.ap, AF.Silu, bias=cb[0:128, i:i + 1], scale=1.0,
                reads=[acc, cb], writes=[xcb[i]])
            b.dma("sync", xc_out[i * 128:(i + 1) * 128, t0:t0 + tl], xcb[i].ap,
                  reads=[xcb[i]])
        fns = [(lambda e, d=d, tl=tl: e.matmul(ps_x[:, 0:tl],
                                               wx[:, d * 96:(d + 1) * 96],
                                               xcb[d].ap, start=(d == 0),
                                               stop=(d == 7)))
               for d in range(8)]
        b.emit_many("tensor", fns, reads=[wx, *xcb], writes=[ps_x])
        vcopy(b, xdbl_sb[:, t0:t0 + tl], ps_x[0:96, 0:tl], reads=[ps_x],
              writes=[xdbl_sb])
    for hh in range(2):
        b.dma("sync", xdbl_out[:, hh * 2048:(hh + 1) * 2048],
              xdbl_sb[:, hh * 2048:(hh + 1) * 2048], reads=[xdbl_sb])
    return b.finish()


def build_ssmb_B():
    b = B("ssmb_B").open()
    T = 4096
    xc_in = b.inp("xc", [1024, T], BF16)
    z_in = b.inp("z", [1024, T], BF16)
    xd0 = b.inp("xd0", [96, T], F32)
    xd1 = b.inp("xd1", [96, T], F32)
    wdtT = b.inp("wdtT", [64, 1024], F32)
    bdt_in = b.inp("bdt", [128, 8], F32)
    alog_in = b.inp("alog", [128, 128], F32)
    dparam_in = b.inp("dparam", [128, 8], F32)
    woutT = b.inp("woutT", [1024, 1024], F32)
    yout = b.out("yout", [1024, T], F32)
    com = Common(b)

    bdt = b.sbuf("bdt_sb", [128, 8], F32)
    b.dma("sync", bdt, bdt_in, writes=[bdt])
    dpar = b.sbuf("dpar_sb", [128, 8], F32)
    b.dma("sync", dpar, dparam_in, writes=[dpar])
    alog = b.sbuf("alog_sb", [128, 128], F32)
    b.dma("sync", alog, alog_in, writes=[alog])
    nasc = b.sbuf("nasc", [128, 128], F32)
    act(b, nasc.ap, alog.ap, AF.Exp, reads=[alog], writes=[nasc])
    ts(b, "vector", nasc.ap, nasc.ap, -1.0, None, ALU.mult, reads=[nasc],
       writes=[nasc])
    wdt = b.sbuf("wdt_sb", [64, 1024], BF16)
    b.dma("gpsimd", wdt, wdtT, writes=[wdt])

    xdbl = b.sbuf("xdbl", [96, T], F32)
    xdt = b.sbuf("xdt", [96, T], F32)
    b.dma("sync", xdbl.ap, xd0, writes=[xdbl])
    b.dma("sync", xdt.ap, xd1, writes=[xdt])
    tt(b, xdbl.ap, xdbl.ap, xdt.ap, ALU.add, reads=[xdbl, xdt], writes=[xdbl])
    dtl_bf = b.sbuf("dtl_bf", [64, T], BF16)
    vcopy(b, dtl_bf.ap, xdbl[0:64, :], reads=[xdbl], writes=[dtl_bf])
    bmc = b.sbuf("bmc", [32, T], BF16)
    bmc32 = b.sbuf("bmc32", [32, T], F32)
    b.dma("sync", bmc32.ap, xdbl[64:96, :], reads=[xdbl], writes=[bmc32])
    vcopy(b, bmc.ap, bmc32.ap, reads=[bmc32], writes=[bmc])

    dt_dram = b.dram("dt_dram", [1024, T], BF16)
    pool = [b.psum(f"mmp{k}", [128, 512], F32) for k in range(2)]
    spt = b.sbuf("spt", [128, 512], F32)
    prst = [b.sbuf(f"prst{k}", [128, 512], BF16) for k in range(2)]
    for ci, (t0, tl) in enumerate(tiles(T, 512)):
        for oi in range(8):
            ps = pool[oi % 2]
            b.emit("tensor", lambda e, oi=oi, ps=ps, t0=t0, tl=tl: e.matmul(
                ps[:, 0:tl], wdt[:, oi * 128:(oi + 1) * 128],
                dtl_bf[:, t0:t0 + tl], start=True, stop=True),
                reads=[wdt, dtl_bf], writes=[ps])
            pr = prst[oi % 2]
            ts(b, "vector", spt[:, 0:tl], ps[:, 0:tl], bdt[:, oi:oi + 1], None,
               ALU.add, reads=[ps, bdt], writes=[spt])
            act(b, spt[:, 0:tl], spt[:, 0:tl], AF.Exp, reads=[spt], writes=[spt])
            act(b, pr[:, 0:tl], spt[:, 0:tl], AF.Ln, bias=1.0, scale=1.0,
                reads=[spt], writes=[pr])
            b.dma("sync", dt_dram[:][oi * 128:(oi + 1) * 128, t0:t0 + tl],
                  pr[:, 0:tl], reads=[pr], writes=[dt_dram])

    TB = 1024
    xib = [b.sbuf(f"xib{i}", [128, TB], BF16) for i in range(8)]
    dtb_ = [b.sbuf(f"dtb{i}", [128, TB], BF16) for i in range(8)]
    v2 = [b.sbuf(f"v2_{i}", [128, TB], BF16) for i in range(8)]
    yacc = [b.sbuf(f"yacc{i}", [128, TB], F32) for i in range(8)]
    carry = [b.sbuf(f"carry{i}", [128, 16], F32) for i in range(8)]
    a_t = b.sbuf("a_t", [128, TB], F32)
    u_t = b.sbuf("u_t", [128, TB], F32)
    h_t = b.sbuf("h_t", [128, TB], F32)
    nwm = b.sbuf("nwm", [128, TB], F32)
    brow = b.sbuf("brow", [1, TB], BF16)
    crow = b.sbuf("crow", [1, TB], BF16)
    bm_bc = b.psum("bm_bc", [128, TB], F32)
    c_bc = b.psum("c_bc", [128, TB], F32)
    zs = b.sbuf("zs", [128, TB], F32)
    outb = [b.sbuf(f"outb{k}", [128, TB], BF16) for k in range(2)]
    for i in range(8):
        b.emit("gpsimd", lambda e, i=i: e.memset(carry[i].ap, 0.0),
               writes=[carry[i]])
    wout = [b.sbuf(f"wout{d}", [128, 1024], BF16) for d in range(8)]
    for d in range(8):
        b.dma("gpsimd", wout[d].ap, woutT[d * 128:(d + 1) * 128, :],
              writes=[wout[d]])
    yst = [b.sbuf(f"yst{k}", [128, 512], F32) for k in range(2)]

    for ti, (t0, tl) in enumerate(tiles(T, TB)):
        for i in range(8):
            b.dma("sync", xib[i].ap, xc_in[i * 128:(i + 1) * 128, t0:t0 + tl],
                  writes=[xib[i]])
            b.dma("sync", dtb_[i].ap, dt_dram[:][i * 128:(i + 1) * 128, t0:t0 + tl],
                  reads=[dt_dram], writes=[dtb_[i]])
            tt(b, v2[i].ap, dtb_[i].ap, xib[i].ap, ALU.mult,
               reads=[dtb_[i], xib[i]], writes=[v2[i]])
        for s in range(16):
            b.dma("sync", brow.ap, bmc[s:s + 1, t0:t0 + tl], reads=[bmc],
                  writes=[brow])
            b.dma("sync", crow.ap, bmc[16 + s:17 + s, t0:t0 + tl], reads=[bmc],
                  writes=[crow])
            bcast_row_bf(b, com, brow, brow.ap, bm_bc)
            bcast_row_bf(b, com, crow, crow.ap, c_bc)
            for d in range(8):
                col = d * 16 + s
                act(b, a_t.ap, dtb_[d].ap, AF.Exp, bias=0.0,
                    scale=nasc[0:128, col:col + 1], reads=[dtb_[d], nasc],
                    writes=[a_t])
                tt(b, u_t.ap, v2[d].ap, bm_bc[0:128, 0:tl], ALU.mult,
                   reads=[v2[d], bm_bc], writes=[u_t])
                b.emit("vector", lambda e, d=d, s=s: e.tensor_tensor_scan(
                    h_t.ap, a_t.ap, u_t.ap, carry[d][:, s:s + 1],
                    ALU.mult, ALU.add),
                    reads=[a_t, u_t, carry[d]], writes=[h_t])
                vcopy(b, carry[d][:, s:s + 1], h_t[:, tl - 1:tl],
                      reads=[h_t, carry[d]], writes=[carry[d]])
                if s == 0:
                    tt(b, yacc[d].ap, h_t.ap, c_bc[0:128, 0:tl], ALU.mult,
                       reads=[h_t, c_bc], writes=[yacc[d]])
                else:
                    tt(b, nwm.ap, h_t.ap, c_bc[0:128, 0:tl], ALU.mult,
                       reads=[h_t, c_bc], writes=[nwm])
                    tt(b, yacc[d].ap, yacc[d].ap, nwm.ap, ALU.add,
                       reads=[yacc[d], nwm], writes=[yacc[d]])
        for i in range(8):
            stt(b, yacc[i].ap, xib[i].ap, dpar[:, i:i + 1], yacc[i].ap,
                ALU.mult, ALU.add, reads=[xib[i], dpar, yacc[i]],
                writes=[yacc[i]])
        for i in range(8):
            b.dma("sync", xib[i].ap, z_in[i * 128:(i + 1) * 128, t0:t0 + tl],
                  writes=[xib[i]])
        for i in range(8):
            ob = outb[i % 2]
            act(b, zs.ap, xib[i].ap, AF.Silu, reads=[xib[i]], writes=[zs])
            tt(b, ob.ap, yacc[i].ap, zs.ap, ALU.mult, reads=[yacc[i], zs],
               writes=[ob])
            vcopy(b, v2[i].ap, ob.ap, reads=[ob], writes=[v2[i]])
        for (q0, ql) in tiles(tl, 512):
            for oi in range(8):
                ps = pool[oi % 2]
                fns = [(lambda e, d=d, oi=oi, ps=ps, q0=q0, ql=ql: e.matmul(
                    ps[:, 0:ql], wout[d][:, oi * 128:(oi + 1) * 128],
                    v2[d][:, q0:q0 + ql], start=(d == 0), stop=(d == 7)))
                    for d in range(8)]
                b.emit_many("tensor", fns, reads=[*wout, *v2], writes=[ps])
                k = oi % 2
                vcopy(b, yst[k].ap[:, 0:ql], ps[:, 0:ql], reads=[ps],
                      writes=[yst[k]])
                b.dma("sync", yout[oi * 128:(oi + 1) * 128,
                                   t0 + q0:t0 + q0 + ql],
                      yst[k][:, 0:ql], reads=[yst[k]])
    return b.finish()


def build_ssmb_C():
    b = B("ssmb_C").open()
    TQ = 2048
    p0 = b.inp("p0", [1024, TQ], F32)
    p1 = b.inp("p1", [1024, TQ], F32)
    x_res = b.inp("x_res", [1024, TQ], F32)
    xnew = b.out("xnew", [1024, TQ], F32)
    t0b = [b.sbuf(f"t0_{k}", [128, 2048], F32) for k in range(2)]
    t1b = [b.sbuf(f"t1_{k}", [128, 2048], F32) for k in range(2)]
    for i in range(8):
        k = i % 2
        b.dma("sync", t0b[k].ap, p0[i * 128:(i + 1) * 128, :], writes=[t0b[k]])
        b.dma("sync", t1b[k].ap, p1[i * 128:(i + 1) * 128, :], writes=[t1b[k]])
        tt(b, t0b[k].ap, t0b[k].ap, t1b[k].ap, ALU.add, reads=[t0b[k], t1b[k]],
           writes=[t0b[k]])
        b.dma("sync", t1b[k].ap, x_res[i * 128:(i + 1) * 128, :], writes=[t1b[k]])
        tt(b, t0b[k].ap, t0b[k].ap, t1b[k].ap, ALU.add, reads=[t0b[k], t1b[k]],
           writes=[t0b[k]])
        b.dma("sync", xnew[i * 128:(i + 1) * 128, :], t0b[k].ap, reads=[t0b[k]])
    return b.finish()


def host_ssmb(xs, wd):
    in_maps = []
    for c in range(8):
        bb, h = c // 2, c % 2
        x_full = np.ascontiguousarray(
            np.concatenate([xs[2 * bb], xs[2 * bb + 1]], axis=1))
        in_maps.append(dict(
            x_full=x_full,
            winT_xs=wd["winT"][:, h * 1024:(h + 1) * 1024],
            winT_z=wd["winT"][:, 2048 + h * 1024:2048 + (h + 1) * 1024],
            convw=wd["convw"][h], convb=wd["convb"][h], wxT=wd["wxT"][h]))
    ra = _run("ssmb_A", build_ssmb_A, in_maps)
    in_maps2 = []
    for c in range(8):
        bb, h = c // 2, c % 2
        in_maps2.append(dict(
            xc=np.asarray(ra[c]["xc"]), z=np.asarray(ra[c]["z"]),
            xd0=np.asarray(ra[2 * bb]["xdbl"]),
            xd1=np.asarray(ra[2 * bb + 1]["xdbl"]),
            wdtT=wd["wdtT"][h], bdt=wd["bdt"][h], alog=wd["alog"][h],
            dparam=wd["dparam"][h], woutT=wd["woutT"][h]))
    rb = _run("ssmb_B", build_ssmb_B, in_maps2)
    in_maps3 = []
    for c in range(8):
        bb, h = c // 2, c % 2
        sl = np.s_[:, h * 2048:(h + 1) * 2048]
        in_maps3.append(dict(
            p0=np.ascontiguousarray(np.asarray(rb[2 * bb]["yout"])[sl]),
            p1=np.ascontiguousarray(np.asarray(rb[2 * bb + 1]["yout"])[sl]),
            x_res=xs[c]))
    rc = _run("ssmb_C", build_ssmb_C, in_maps3)
    return [np.asarray(rc[c]["xnew"]) for c in range(8)]



def _to_fm_parts(x):
    parts = []
    for bb in range(4):
        for h in range(2):
            parts.append(np.ascontiguousarray(
                x[bb, h * 2048:(h + 1) * 2048, :].T).astype(np.float32))
    return parts


def _from_fm_parts(parts):
    out = np.zeros((4, 4096, 1024), np.float32)
    for bb in range(4):
        for h in range(2):
            out[bb, h * 2048:(h + 1) * 2048, :] = parts[2 * bb + h].T
    return out


def _p8(v):
    return [np.ascontiguousarray(v[h * 1024:(h + 1) * 1024].reshape(8, 128).T)
            for h in range(2)]


def _alog_lay(a):
    return [np.ascontiguousarray(
        a[h * 1024:(h + 1) * 1024].reshape(8, 128, 16)
        .transpose(1, 0, 2).reshape(128, 128)) for h in range(2)]


def _conv_lay(cw):
    return [np.ascontiguousarray(
        cw[h * 1024:(h + 1) * 1024].reshape(8, 128, 4)
        .transpose(1, 0, 2).reshape(128, 32)) for h in range(2)]


def _cT(a):
    return np.ascontiguousarray(np.asarray(a, np.float32).T)


def _r128(v):
    return np.ascontiguousarray(np.asarray(v, np.float32).reshape(8, 128).T)


def kernel(**inputs):
    ii = {k: np.asarray(v, np.float32) for k, v in inputs.items()}
    xs = _to_fm_parts(ii["L0_attn_a_x"])

    dq = host_dequant(ii)
    ffn_w = {
        0: {"uT": dq["u0T"], "gT": dq["g0T"], "dT": dq["d0T"]},
        1: {"uT": dq["u1T"], "gT": dq["g1T"], "dT": dq["d1T"]},
    }

    attn_a_w = dict(
        qwT=_cT(ii["L0_attn_a_q_w"]), kwT=_cT(ii["L0_attn_a_k_w"]),
        vwT=_cT(ii["L0_attn_a_v_w"]), owT=_cT(ii["L0_attn_a_o_w"]),
        qb=_r128(ii["L0_attn_a_q_b"]), kb=_r128(ii["L0_attn_a_k_b"]),
        vb=_r128(ii["L0_attn_a_v_b"]), ob=_r128(ii["L0_attn_a_o_b"]))

    attn_b_w = dict(
        wq=np.ascontiguousarray(ii["L0_attn_b_w_q"]),
        wkv=np.ascontiguousarray(ii["L0_attn_b_w_kv"]),
        wo=np.ascontiguousarray(ii["L0_attn_b_w_out"]),
        bo=_r128(ii["L0_attn_b_b_out"]),
        scl=np.ascontiguousarray(ii["L0_attn_b_scale"].reshape(1, 16)))

    ssma_w = dict(
        winT=_cT(ii["L1_attn_a_Win"]), wdtT=_cT(ii["L1_attn_a_Wdt"]),
        wbcT=np.ascontiguousarray(np.concatenate(
            [ii["L1_attn_a_WB"].T, ii["L1_attn_a_WC"].T], axis=1)),
        alog=_alog_lay(ii["L1_attn_a_A_log"]), bdt=_p8(ii["L1_attn_a_bdt"]),
        dtb=_p8(ii["L1_attn_a_dt_bias"]),
        gam=np.ascontiguousarray(ii["L1_attn_a_gamma"].reshape(16, 128).T),
        bet=np.ascontiguousarray(ii["L1_attn_a_beta"].reshape(16, 128).T),
        woutT=_cT(ii["L1_attn_a_Wout"]))

    wxTf = _cT(ii["L1_attn_b_W_x"])
    wdtTf = _cT(ii["L1_attn_b_W_dt"])
    woutTf = _cT(ii["L1_attn_b_W_out"])
    ssmb_w = dict(
        winT=_cT(ii["L1_attn_b_W_in"]),
        convw=_conv_lay(ii["L1_attn_b_conv_w"]),
        convb=_p8(ii["L1_attn_b_conv_b"]),
        wxT=[np.ascontiguousarray(wxTf[h * 1024:(h + 1) * 1024])
             for h in range(2)],
        wdtT=[np.ascontiguousarray(wdtTf[:, h * 1024:(h + 1) * 1024])
              for h in range(2)],
        bdt=_p8(ii["L1_attn_b_b_dt"]), alog=_alog_lay(ii["L1_attn_b_A_log"]),
        dparam=_p8(ii["L1_attn_b_D_param"]),
        woutT=[np.ascontiguousarray(woutTf[h * 1024:(h + 1) * 1024])
               for h in range(2)])

    xs = host_attn_a(xs, attn_a_w)
    xs = host_ffn(xs, ffn_w[0])
    xs = host_attn_b(xs, attn_b_w)
    xs = host_ffn(xs, ffn_w[0])
    xs = host_ssma(xs, ssma_w)
    xs = host_ffn(xs, ffn_w[1])
    xs = host_ssmb(xs, ssmb_w)
    xs = host_ffn(xs, ffn_w[1])

    return _from_fm_parts(xs)
